# version 06
# speedup vs baseline: 1.1177x; 1.0090x over previous
import contextlib
import numpy as np
import ml_dtypes

import concourse.bass as bass
import concourse.mybir as mybir
from concourse.bass_utils import run_bass_kernel_spmd
from concourse.bass import IndirectOffsetOnAxis

F32 = mybir.dt.float32
BF16 = mybir.dt.bfloat16
I32 = mybir.dt.int32
U32 = mybir.dt.uint32
AF = mybir.ActivationFunctionType
ALU = mybir.AluOpType
AX = mybir.AxisListType

NCORES = 8
B_LOC = 2
S = 2048
D = 1024
NT_B = S // 128
NT = B_LOC * NT_B
NTOK = NT * 128
NEG = -1e9

BF_NP = ml_dtypes.bfloat16

ENGINE_SEM = {"vector": "dve", "scalar": "act", "tensor": "pe",
              "gpsimd": "pool", "sync": "ds"}
DMA_SEM = {"gpsimd": "dg", "sync": "ds2", "scalar": "dscl", "vector": "dv2"}


class Reg:
    __slots__ = ("w", "rs", "name")

    def __init__(self, name=""):
        self.w = None
        self.rs = []
        self.name = name


class Prog:
    def __init__(self, nc, block, stack):
        self.nc = nc
        self.block = block
        self.stack = stack
        self.sems = {}
        self.seen = {}
        self.nops = 0

    def sem(self, name):
        if name not in self.sems:
            h = self.stack.enter_context(self.nc.semaphore(name))
            self.sems[name] = [h, 0]
        return name

    def tok(self, name):
        self.sem(name)
        return (name, self.sems[name][1])

    def _raw_run(self, eng, fn, deps, inc, inc_k):
        waits = []
        for (s, v) in deps:
            if v <= 0:
                continue
            key = (eng, s)
            if self.seen.get(key, 0) < v:
                waits.append((self.sems[s][0], v))
                self.seen[key] = v
        token = None
        if inc is not None:
            self.sem(inc)
            self.sems[inc][1] += inc_k
            token = (inc, self.sems[inc][1])
            inc_h = self.sems[inc][0]

        def body(e):
            for (h, v) in waits:
                e.wait_ge(h, v)
            last = fn(e)
            if inc is not None:
                if last is None:
                    last = e.engine_nop() if hasattr(e, "engine_nop") else e.nop()
                last.then_inc(inc_h, inc_k)

        getattr(self.block, eng)(body)
        self.nops += 1
        return token

    def run(self, eng, fn, reads=(), writes=(), dma=False, inc_k=None):
        deps = []
        for r in reads:
            if isinstance(r, Reg):
                if r.w is not None:
                    deps.append(r.w)
            elif r is not None:
                deps.append(r)
        for r in writes:
            if r.w is not None:
                deps.append(r.w)
            deps.extend(r.rs)
        sem = DMA_SEM[eng] if dma else ENGINE_SEM[eng]
        k = inc_k if inc_k is not None else (16 if dma else 1)
        tok = self._raw_run(eng, fn, tuple(deps), sem, k)
        for r in writes:
            r.w = tok
            r.rs = []
        for r in reads:
            if isinstance(r, Reg):
                r.rs.append(tok)
        return tok


CONST_SCALARS = [0.0, 1.0, -1.0, 0.5, -0.5, 2.0, -2.0, 1e-5, 1e-6, 1e-9,
                 -0.1, 0.1, float(1.0 + np.pi / 2), -3.2, float(1.0 / 7.5),
                 float(-1.0 + 1e-7), float(1.0 - 1e-7), float(1.0 - 2e-7),
                 0.125, -0.125, 0.0625, -0.0625, float(1.0 / 1024.0),
                 float(-1.0 / 1024.0), 1e-12, 320.0, 319.0, 640.0, -640.0,
                 -1e9, 2e-7, 1e-30]


def build_host_consts():
    CW = 688
    c = np.zeros((128, CW), np.float32)
    for i, v in enumerate(CONST_SCALARS):
        c[:, i] = np.float32(v)
    c[:, 32:160] = np.eye(128, dtype=np.float32)
    c[:, 160:288] = np.triu(np.ones((128, 128), np.float32), 1)
    c[:, 288:416] = np.triu(np.ones((128, 128), np.float32), 1) * NEG
    c[:, 416:544] = np.triu(np.ones((128, 128), np.float32), 0)
    c[:, 544:552] = np.arange(8, dtype=np.float32)[None, :]
    c[:, 552:680] = 1.0
    c[:, 680] = np.arange(128, dtype=np.float32)
    cb = np.zeros((128, 256), BF_NP)
    cb[:, 0:128] = np.eye(128, dtype=np.float32).astype(BF_NP)
    cb[:, 128:256] = np.ones((128, 128), np.float32).astype(BF_NP)
    return c, cb


class Consts:
    def __init__(self, nc, stack):
        CW = 688
        self.cf = stack.enter_context(nc.sbuf_tensor("cst_f", [128, CW], F32))
        self.cb = stack.enter_context(nc.sbuf_tensor("cst_b", [128, 256], BF16))
        self.dram_f = nc.dram_tensor("consts_f", [128, CW], F32, kind="ExternalInput").ap()
        self.dram_b = nc.dram_tensor("consts_b", [128, 256], BF16, kind="ExternalInput").ap()
        for i, v in enumerate(CONST_SCALARS):
            nc.const_aps.register(F32, float(np.float32(v)), self.cf[:, i:i + 1])
        self.reg = Reg("consts")

    def load(self, P):
        P.run("sync", lambda e: e.dma_start(out=self.cf[:], in_=self.dram_f[:]),
              writes=[self.reg], dma=True)
        P.run("sync", lambda e: e.dma_start(out=self.cb[:], in_=self.dram_b[:]),
              writes=[self.reg], dma=True)
        self.ready = self.reg.w

    def col(self, value):
        i = CONST_SCALARS.index(value)
        return self.cf[:, i:i + 1]

    ident_f = property(lambda self: self.cf[:, 32:160])
    strict_upper = property(lambda self: self.cf[:, 160:288])
    neg_upper = property(lambda self: self.cf[:, 288:416])
    incl_upper = property(lambda self: self.cf[:, 416:544])
    iota8 = property(lambda self: self.cf[:, 544:552])
    ones_f = property(lambda self: self.cf[:, 552:680])
    iota128 = property(lambda self: self.cf[:, 680:681])
    ident_b = property(lambda self: self.cb[:, 0:128])
    ones_b = property(lambda self: self.cb[:, 128:256])


GLOBAL_BUFS = {"x_res", "scr4k", "cst_f", "cst_b"}
ARENA_COLS = 37184


class Env:
    def __init__(self, nc, block, stack):
        self.nc = nc
        self.block = block
        self.stack = stack
        self.sb = {}
        self.ps = {}
        self.regs = {}
        self.dr = {}
        self.arena = None
        self.arena_off = 0
        self.arena_hw = 0

    def reset_block_scope(self):
        for name in [n for n in self.sb if n not in GLOBAL_BUFS]:
            del self.sb[name]
        self.arena_off = 0

    def dram(self, name, shape, dtype):
        if name not in self.dr:
            self.dr[name] = self.nc.dram_tensor(name, list(shape), dtype,
                                                kind="Internal").ap()
        return self.dr[name]

    def sbuf(self, name, shape, dtype):
        if name in self.sb:
            return self.sb[name]
        if name in GLOBAL_BUFS:
            self.sb[name] = self.stack.enter_context(
                self.nc.sbuf_tensor(name, list(shape), dtype))
            return self.sb[name]
        if self.arena is None:
            self.arena = self.stack.enter_context(
                self.nc.sbuf_tensor("arena", [128, ARENA_COLS], BF16))
        esz = {F32: 2, I32: 2, U32: 2, BF16: 1}[dtype]
        cols_bf = shape[1] * esz
        off = (self.arena_off + 63) & ~63
        assert off + cols_bf <= ARENA_COLS,             f"arena overflow at {name}: {off}+{cols_bf}"
        self.arena_off = off + cols_bf
        self.arena_hw = max(self.arena_hw, self.arena_off)
        view = self.arena[:, off:off + cols_bf]
        if dtype != BF16:
            view = view.bitcast(dtype)
        if shape[0] != 128:
            view = view[0:shape[0], :]
        self.sb[name] = view
        return self.sb[name]

    def psum(self, name, shape, dtype):
        if name not in self.ps:
            uname = f"{name}_s{self.scope_id}" if getattr(self, "scope_id", 0) else name
            self.ps[name] = self.stack.enter_context(
                self.nc.psum_tensor(uname, list(shape), dtype))
        return self.ps[name]

    def reg(self, name):
        if name not in self.regs:
            self.regs[name] = Reg(name)
        return self.regs[name]


def setup_psum_banks(env):
    env.P0 = env.psum("psum0", [128, 512], F32)
    env.P1 = env.psum("psum1", [128, 512], F32)
    env.P2 = env.psum("psum2", [128, 512], F32)
    env.P3 = env.psum("psum3", [128, 512], F32)
    env.P4 = env.psum("psum4", [128, 512], F32)
    env.P5 = env.psum("psum5", [128, 512], F32)
    env.T0 = env.psum("psumT0", [128, 1024], BF16)
    env.T1 = env.psum("psumT1", [128, 1024], BF16)
    env.rP = [env.reg(f"rP{i}") for i in range(6)]
    env.rT = [env.reg("rT0"), env.reg("rT1")]
    env.PS = [env.P0, env.P1, env.P2, env.P3, env.P4, env.P5]
    env.TS = [env.T0, env.T1]


def setup_psum(env):
    env.scr4k = env.sbuf("scr4k", [128, 1024], F32)
    env.rscr = env.reg("scr4k")


def transpose128(P, env, C, dst_ap, src_ap, reads, dst_reg, ti, scale_col=None,
                 scale_reads=()):
    if src_ap.dtype == BF16:
        bank, breg = env.TS[ti], env.rT[ti]
        ident = C.ident_b
    else:
        bank, breg = (env.P4, env.rP[4]) if ti == 0 else (env.P5, env.rP[5])
        ident = C.ident_f
    p, f = src_ap.shape
    P.run("tensor", lambda e: e.transpose(bank[0:f, 0:p], src_ap, ident[0:p, 0:p]),
          reads=list(reads) + [C.reg], writes=[breg])
    if scale_col is not None:
        return P.run("vector",
                     lambda e: e.tensor_scalar(out=dst_ap, in0=bank[0:f, 0:p],
                                               scalar1=scale_col, scalar2=None,
                                               op0=ALU.mult),
                     reads=[breg] + list(scale_reads), writes=[dst_reg])
    return P.run("vector", lambda e: e.tensor_copy(out=dst_ap, in_=bank[0:f, 0:p]),
                 reads=[breg], writes=[dst_reg])


def emit_layernorm(P, env, C, x_sb, xregs, tag="", rms=False):
    nm = env.sbuf(f"ln_nm{tag}", [128, NT], F32)
    rstd = env.sbuf(f"ln_rs{tag}", [128, NT], F32)
    ssum = env.sbuf(f"ln_ss{tag}", [128, NT], F32)
    sq = env.scr4k
    rsq = env.rscr
    rnm, rrs, rss = env.reg(f"nm{tag}"), env.reg(f"rs{tag}"), env.reg(f"ss{tag}")
    for i in range(NT):
        xt = x_sb[:, i * 1024:(i + 1) * 1024]
        if not rms:
            P.run("vector",
                  lambda e, xt=xt, i=i: e.reduce_sum(out=ssum[:, i:i + 1], in_=xt, axis=AX.X),
                  reads=[xregs[i]], writes=[rss])
        P.run("scalar",
              lambda e, xt=xt, i=i: e.activation(out=sq[:], in_=xt, func=AF.Square,
                                                 accum_out=nm[:, i:i + 1]),
              reads=[xregs[i]], writes=[rsq, rnm])
    if rms:
        P.run("scalar",
              lambda e: e.activation(out=rstd[:], in_=nm[:], func=AF.Sqrt,
                                     scale=float(1.0 / 1024.0), bias=C.col(1e-6)),
              reads=[rnm, C.reg], writes=[rrs])
        P.run("vector", lambda e: e.reciprocal(out=rstd[:], in_=rstd[:]),
              reads=[], writes=[rrs])
        return None, rstd, rrs
    P.run("vector",
          lambda e: e.tensor_scalar(out=rstd[:], in0=nm[:], scalar1=float(1.0 / 1024.0),
                                    scalar2=None, op0=ALU.mult),
          reads=[rnm], writes=[rrs])
    P.run("vector",
          lambda e: e.tensor_scalar(out=nm[:], in0=ssum[:], scalar1=float(-1.0 / 1024.0),
                                    scalar2=None, op0=ALU.mult),
          reads=[rss], writes=[rnm])
    P.run("vector",
          lambda e: e.scalar_tensor_tensor(out=ssum[:], in0=nm[:], scalar=-1.0,
                                           in1=nm[:], op0=ALU.mult, op1=ALU.mult),
          reads=[rnm], writes=[rss])
    P.run("vector",
          lambda e: e.tensor_tensor(out=rstd[:], in0=rstd[:], in1=ssum[:], op=ALU.add),
          reads=[rss], writes=[rrs])
    P.run("scalar",
          lambda e: e.activation(out=rstd[:], in_=rstd[:], func=AF.Sqrt, bias=C.col(1e-5)),
          reads=[C.reg], writes=[rrs])
    P.run("vector", lambda e: e.reciprocal(out=rstd[:], in_=rstd[:]), writes=[rrs])
    return nm, rstd, rrs


def emit_norm_tile(P, env, C, x_sb, xregs, nm, rstd, rrs, i, out_ap, out_reg):
    xt = x_sb[:, i * 1024:(i + 1) * 1024]
    if nm is None:
        return P.run("vector",
                     lambda e: e.tensor_scalar(out=out_ap, in0=xt,
                                               scalar1=rstd[:, i:i + 1], scalar2=None,
                                               op0=ALU.mult),
                     reads=[xregs[i], rrs], writes=[out_reg])
    return P.run("vector",
                 lambda e: e.tensor_scalar(out=out_ap, in0=xt, scalar1=nm[:, i:i + 1],
                                           scalar2=rstd[:, i:i + 1], op0=ALU.add,
                                           op1=ALU.mult),
                 reads=[xregs[i], rrs], writes=[out_reg])


def emit_xnfm(P, env, C, x_sb, xregs, nm, rstd, rrs, b, xn_fm, rxn,
              gamma_bc=None, rgam=None):
    scr = env.scr4k[:].bitcast(BF16)
    rscr = [env.rscr, env.rscr]
    for il in range(NT_B):
        i = b * NT_B + il
        half = il % 2
        dst = scr[:, half * 1024:(half + 1) * 1024]
        emit_norm_tile(P, env, C, x_sb, xregs, nm, rstd, rrs, i, dst, rscr[half])
        if gamma_bc is not None:
            P.run("vector",
                  lambda e, dst=dst: e.tensor_tensor(out=dst, in0=dst, in1=gamma_bc,
                                                     op=ALU.mult),
                  reads=[rgam], writes=[rscr[half]])
        bank, breg = env.TS[il % 2], env.rT[il % 2]
        for c in range(8):
            P.run("tensor",
                  lambda e, c=c, dst=dst, bank=bank: e.transpose(
                      bank[:, c * 128:(c + 1) * 128], dst[:, c * 128:(c + 1) * 128],
                      C.ident_b),
                  reads=[rscr[half], C.reg], writes=[breg])
        P.run("vector",
              lambda e, il=il, bank=bank: e.tensor_copy(
                  out=xn_fm[:, il * 128:(il * 128) + 1024]
                      .rearrange("p (c w) -> p c w", w=128) if False else
                  xn_fm[:, 0:8 * S].rearrange("p (c t) -> p c t", t=S)
                      [:, :, il * 128:(il + 1) * 128],
                  in_=bank[:, 0:1024].rearrange("p (c w) -> p c w", w=128)),
              reads=[breg], writes=[rxn])
    return rxn


class WSlice:

    def __init__(self, P, env, name, width=1024, nbuf=2):
        self.P, self.env = P, env
        self.width = width
        self.stage = env.sbuf(f"wsl_{name}", [128, nbuf * width], BF16)
        self.regs = [env.reg(f"wsl_{name}{i}") for i in range(nbuf)]
        self.nbuf = nbuf
        self.cur = 0

    def load(self, ap):
        bi = self.cur
        self.cur = (self.cur + 1) % self.nbuf
        p, f = ap.shape
        view = self.stage[:, bi * self.width:bi * self.width + self.width]
        self.P.run("gpsimd",
                   lambda e: e.dma_start(out=view[0:p, 0:f], in_=ap),
                   writes=[self.regs[bi]], dma=True)
        return view, self.regs[bi]


def wT_load(P, env, C, ws, w_ap, o0, osz, dst_sb, dst_reg, dst_off=0,
            gamma_cols=None, rgc=None):
    dcols = w_ap.shape[1]
    ndc = dcols // 128
    for oc in range(ceil_div(osz, 128)):
        owid = min(128, osz - oc * 128)
        view, vreg = ws.load(w_ap[o0 + oc * 128:o0 + oc * 128 + owid, :])
        bank, breg = env.TS[oc % 2], env.rT[oc % 2]
        for dc in range(ndc):
            P.run("tensor",
                  lambda e, dc=dc, bank=bank, view=view, owid=owid: e.transpose(
                      bank[0:owid, dc * 128:dc * 128 + owid] if False else
                      bank[:, dc * 128:dc * 128 + owid],
                      view[0:owid, dc * 128:(dc + 1) * 128], C.ident_b[0:owid, 0:owid]),
                  reads=[vreg, C.reg], writes=[breg])
        out_ap = dst_sb[:, dst_off:dst_off + ndc * osz] \
            .rearrange("p (dc o) -> p dc o", o=osz)[:, :, oc * 128:oc * 128 + owid]
        in_ap = bank[:, 0:ndc * 128].rearrange("p (dc w) -> p dc w", w=128)[:, :, 0:owid]
        if gamma_cols is not None:
            for dc in range(ndc):
                P.run("vector",
                      lambda e, dc=dc, bank=bank, owid=owid: e.tensor_scalar(
                          out=dst_sb[:, dst_off + dc * osz + oc * 128:
                                     dst_off + dc * osz + oc * 128 + owid],
                          in0=bank[:, dc * 128:dc * 128 + owid],
                          scalar1=gamma_cols[:, dc:dc + 1], scalar2=None, op0=ALU.mult),
                      reads=[breg, rgc], writes=[dst_reg])
        else:
            P.run("vector",
                  lambda e, out_ap=out_ap, in_ap=in_ap: e.tensor_copy(
                      out=out_ap, in_=in_ap),
                  reads=[breg], writes=[dst_reg])
    return dst_reg


def make_delta_adder(P, env, C, x_sb, xregs):
    sfm = env.scr4k
    rfm = [env.rscr, env.rscr]

    def cb(b, oc, tch, psu, preg):
        half = tch % 2
        buf = sfm[:, half * 512:(half + 1) * 512]
        P.run("vector", lambda e: e.tensor_copy(out=buf, in_=psu[:, 0:512]),
              reads=[preg], writes=[rfm[half]])
        bank, breg = (env.P4, env.rP[4]) if half == 0 else (env.P5, env.rP[5])
        for j in range(4):
            P.run("tensor",
                  lambda e, j=j, buf=buf, bank=bank: e.transpose(
                      bank[:, j * 128:(j + 1) * 128], buf[:, j * 128:(j + 1) * 128],
                      C.ident_f),
                  reads=[rfm[half], C.reg], writes=[breg])
        it0 = b * NT_B + tch * 4
        xv = x_sb.rearrange("p (i d) -> p i d", d=1024)[:, it0:it0 + 4,
                                                       oc * 128:(oc + 1) * 128]
        P.run("vector",
              lambda e, bank=bank: e.tensor_tensor(
                  out=xv, in0=xv,
                  in1=bank[:, 0:512].rearrange("p (j w) -> p j w", w=128),
                  op=ALU.add),
              reads=[breg], writes=[xregs[it0], xregs[it0 + 1],
                                    xregs[it0 + 2], xregs[it0 + 3]])
    return cb


def emit_mha_causal(P, env, C, x_sb, xregs, delta_cb, getw, wo_ap,
                    nm, rstd, rrs, gamma_bc=None, rgam=None, scale=0.125,
                    gamma_cols=None, rgc=None, ws_nbuf=2):
    HS = S // 2
    NQT = NT_B // 2
    xn_fm = env.sbuf("xn_fm", [128, 8 * S], BF16)
    rxn = env.reg("xn_fm")
    qf = env.sbuf("mha_qf", [128, HS], BF16)
    kf = env.sbuf("mha_kf", [128, S], BF16)
    vt = env.sbuf("mha_vt", [128, S], BF16)
    attn = env.sbuf("mha_attn", [128, 8 * HS], BF16)
    probs = env.sbuf("mha_probs", [128, S], BF16)
    pT = env.sbuf("mha_pT", [128, 1024], BF16)
    ncols = env.sbuf("mha_ncols", [128, NQT + 8], F32)
    dcc = env.sbuf("mha_dc", [128, 8], F32)
    wst = env.sbuf("mha_wst", [128, 2 * 8 * 128], BF16)
    rqf, rkf, rvt = env.reg("qf"), env.reg("kf"), env.reg("vt")
    rattn, rprobs, rpT = env.reg("attn"), env.reg("probs"), env.reg("pT")
    rncols, rdcc = env.reg("ncols"), env.reg("dcc")
    rs0, rs1 = env.reg("wslot0"), env.reg("wslot1")
    ws = WSlice(P, env, "mha", 1024, ws_nbuf)

    for b in range(B_LOC):
        emit_xnfm(P, env, C, x_sb, xregs, nm, rstd, rrs, b, xn_fm, rxn,
                  gamma_bc=gamma_bc, rgam=rgam)
        for half in range(2):
            q0 = half * HS
            for g in range(8):
                wT_load(P, env, C, ws, getw("q", g), 0, 128, wst, rs0, 0,
                        gamma_cols=gamma_cols, rgc=rgc)
                wT_load(P, env, C, ws, getw("k", g), 0, 128, wst, rs1, 8 * 128,
                        gamma_cols=gamma_cols, rgc=rgc)
                for tch in range(HS // 512):
                    pi = tch % 2
                    psu, preg = env.PS[pi], env.rP[pi]
                    def mmq(e, tch=tch, psu=psu, q0=q0):
                        last = None
                        for dc in range(8):
                            last = e.matmul(psu[:], wst[:, dc * 128:(dc + 1) * 128],
                                            xn_fm[:, dc * S + q0 + tch * 512:
                                                  dc * S + q0 + (tch + 1) * 512],
                                            start=(dc == 0), stop=(dc == 7))
                        return last
                    P.run("tensor", mmq, reads=[rs0, rxn], writes=[preg])
                    P.run("scalar",
                          lambda e, tch=tch, psu=psu: e.copy(
                              out=qf[:, tch * 512:(tch + 1) * 512], in_=psu[:]),
                          reads=[preg], writes=[rqf])
                kwmax = q0 + HS
                for tch in range(kwmax // 512):
                    pi = tch % 2
                    psu, preg = env.PS[pi], env.rP[pi]
                    def mmk(e, tch=tch, psu=psu):
                        last = None
                        for dc in range(8):
                            last = e.matmul(psu[:], wst[:, 8 * 128 + dc * 128:
                                                        8 * 128 + (dc + 1) * 128],
                                            xn_fm[:, dc * S + tch * 512:dc * S + (tch + 1) * 512],
                                            start=(dc == 0), stop=(dc == 7))
                        return last
                    P.run("tensor", mmk, reads=[rs1, rxn], writes=[preg])
                    P.run("scalar",
                          lambda e, tch=tch, psu=psu: e.copy(
                              out=kf[:, tch * 512:(tch + 1) * 512], in_=psu[:]),
                          reads=[preg], writes=[rkf])
                wT_load(P, env, C, ws, getw("v", g), 0, 128, wst, rs0, 0,
                        gamma_cols=gamma_cols, rgc=rgc)
                for it in range(kwmax // 128):
                    pi = it % 2
                    psu, preg = env.PS[2 + pi], env.rP[2 + pi]
                    def mmv(e, it=it, psu=psu):
                        last = None
                        for dc in range(8):
                            last = e.matmul(psu[:, 0:128],
                                            xn_fm[:, dc * S + it * 128:dc * S + (it + 1) * 128],
                                            wst[:, dc * 128:(dc + 1) * 128],
                                            start=(dc == 0), stop=(dc == 7))
                        return last
                    P.run("tensor", mmv, reads=[rs0, rxn], writes=[preg])
                    P.run("scalar",
                          lambda e, it=it, psu=psu: e.copy(
                              out=vt[:, it * 128:(it + 1) * 128], in_=psu[:, 0:128]),
                          reads=[preg], writes=[rvt])

                for hloc in range(2):
                    hb = hloc * 64
                    qrows = qf[hb:hb + 64, :]
                    krows = kf[hb:hb + 64, :]
                    P.run("scalar",
                          lambda e, krows=krows, hb=hb, kwmax=kwmax: e.activation(
                              out=probs[hb:hb + 64, 0:kwmax], in_=krows[:, 0:kwmax],
                              func=AF.Square),
                          reads=[rkf], writes=[rprobs])
                    for it in range(kwmax // 128):
                        P.run("tensor",
                              lambda e, it=it, hb=hb: e.matmul(
                                  env.P2[:, it:it + 1],
                                  probs[hb:hb + 64, it * 128:(it + 1) * 128],
                                  C.ones_b[hb:hb + 64, 0:1],
                                  start=True, stop=True),
                              reads=[rprobs, C.reg], writes=[env.rP[2]])
                    P.run("vector",
                          lambda e, kwmax=kwmax: e.reduce_max(
                              out=ncols[:, NQT:NQT + 1], in_=env.P2[:, 0:kwmax // 128],
                              axis=AX.X),
                          reads=[env.rP[2]], writes=[rncols])
                    tb, tbr = env.P4, env.rP[4]
                    P.run("tensor",
                          lambda e, tb=tb: e.transpose(tb[0:1, 0:128],
                                                       ncols[:, NQT:NQT + 1], C.ident_f),
                          reads=[rncols, C.reg], writes=[tbr])
                    P.run("vector",
                          lambda e, tb=tb: e.reduce_max(out=ncols[0:1, NQT + 1:NQT + 2],
                                                        in_=tb[0:1, 0:128], axis=AX.X),
                          reads=[tbr], writes=[rncols])
                    P.run("scalar",
                          lambda e: e.activation(out=ncols[0:1, NQT + 2:NQT + 3],
                                                 in_=ncols[0:1, NQT + 1:NQT + 2],
                                                 func=AF.Sqrt),
                          writes=[rncols])
                    P.run("tensor",
                          lambda e: e.matmul(env.P3[:, 0:1], C.ones_f[0:1, 0:128],
                                             ncols[0:1, NQT + 2:NQT + 3],
                                             start=True, stop=True),
                          reads=[rncols, C.reg], writes=[env.rP[3]])
                    P.run("vector",
                          lambda e: e.tensor_scalar(out=ncols[:, NQT + 3:NQT + 4],
                                                    in0=env.P3[:, 0:1],
                                                    scalar1=float(-scale), scalar2=None,
                                                    op0=ALU.mult),
                          reads=[env.rP[3]], writes=[rncols])
                    P.run("scalar",
                          lambda e, qrows=qrows, hb=hb: e.activation(
                              out=probs[hb:hb + 64, 0:HS], in_=qrows, func=AF.Square),
                          reads=[rqf], writes=[rprobs])
                    for it in range(NQT):
                        P.run("tensor",
                              lambda e, it=it, hb=hb: e.matmul(
                                  env.P2[:, 16 + it:17 + it],
                                  probs[hb:hb + 64, it * 128:(it + 1) * 128],
                                  C.ones_b[hb:hb + 64, 0:1],
                                  start=True, stop=True),
                              reads=[rprobs, C.reg], writes=[env.rP[2]])
                    P.run("scalar",
                          lambda e: e.activation(out=ncols[:, 0:NQT],
                                                 in_=env.P2[:, 16:16 + NQT], func=AF.Sqrt),
                          reads=[env.rP[2]], writes=[rncols])
                    P.run("vector",
                          lambda e: e.tensor_scalar(out=ncols[:, 0:NQT],
                                                    in0=ncols[:, 0:NQT],
                                                    scalar1=ncols[:, NQT + 3:NQT + 4],
                                                    scalar2=-0.1, op0=ALU.mult, op1=ALU.add),
                          writes=[rncols])
                    for ql in range(NQT):
                        qt = half * NQT + ql
                        kw = (qt + 1) * 128
                        nch = ceil_div(kw, 512)
                        for ch in range(nch):
                            w = min(512, kw - ch * 512)
                            pi = ch % 2
                            psu, preg = env.PS[pi], env.rP[pi]
                            P.run("tensor",
                                  lambda e, ql=ql, ch=ch, w=w, psu=psu, qrows=qrows, krows=krows:
                                      e.matmul(psu[:, 0:w],
                                               qrows[:, ql * 128:(ql + 1) * 128],
                                               krows[:, ch * 512:ch * 512 + w],
                                               start=True, stop=True),
                                  reads=[rqf, rkf], writes=[preg])
                            if ch == nch - 1:
                                P.run("vector",
                                      lambda e, w=w, psu=psu: e.tensor_tensor(
                                          out=psu[:, w - 128:w], in0=psu[:, w - 128:w],
                                          in1=C.neg_upper, op=ALU.add),
                                      reads=[C.reg], writes=[preg])
                            P.run("scalar",
                                  lambda e, ch=ch, w=w, ql=ql, psu=psu: e.activation(
                                      out=probs[:, ch * 512:ch * 512 + w], in_=psu[:, 0:w],
                                      func=AF.Exp, scale=float(scale),
                                      bias=ncols[:, ql:ql + 1],
                                      accum_out=dcc[:, ch:ch + 1]),
                                  reads=[preg, rncols], writes=[rprobs, rdcc])
                        P.run("vector",
                              lambda e, nch=nch: e.reduce_sum(out=dcc[:, 4:5],
                                                              in_=dcc[:, 0:nch], axis=AX.X),
                              writes=[rdcc])
                        P.run("vector",
                              lambda e: e.reciprocal(out=dcc[:, 5:6], in_=dcc[:, 4:5]),
                              writes=[rdcc])
                        P.run("vector",
                              lambda e, kw=kw: e.tensor_scalar(
                                  out=probs[:, 0:kw], in0=probs[:, 0:kw],
                                  scalar1=dcc[:, 5:6], scalar2=None, op0=ALU.mult),
                              reads=[rdcc], writes=[rprobs])
                        nkt = qt + 1
                        psa, psareg = env.PS[2 + ql % 2], env.rP[2 + ql % 2]
                        for j0 in range(0, nkt, 4):
                            gn = min(4, nkt - j0)
                            ti = (j0 // 4) % 2
                            bank, breg = env.TS[ti], env.rT[ti]
                            for j in range(j0, j0 + gn):
                                P.run("tensor",
                                      lambda e, j=j, j0=j0, bank=bank: e.transpose(
                                          bank[:, (j - j0) * 128:(j - j0 + 1) * 128],
                                          probs[:, j * 128:(j + 1) * 128], C.ident_b),
                                      reads=[rprobs, C.reg], writes=[breg])
                            P.run("vector",
                                  lambda e, gn=gn, bank=bank: e.tensor_copy(
                                      out=pT[:, 0:gn * 128],
                                      in_=bank[:, 0:gn * 128]),
                                  reads=[breg], writes=[rpT])
                            def mmav(e, j0=j0, gn=gn, hb=hb, psa=psa, nkt=nkt):
                                last = None
                                for j in range(j0, j0 + gn):
                                    last = e.matmul(
                                        psa[hb:hb + 64, 0:128],
                                        vt[:, j * 128 + hb:j * 128 + hb + 64],
                                        pT[:, (j - j0) * 128:(j - j0 + 1) * 128],
                                        start=(j == 0), stop=(j == nkt - 1),
                                        tile_position=(0, hb))
                                return last
                            P.run("tensor", mmav, reads=[rpT, rvt], writes=[psareg])
                        P.run("scalar",
                              lambda e, ql=ql, hb=hb, g=g, psa=psa: e.copy(
                                  out=attn[hb:hb + 64,
                                           g * HS + ql * 128:g * HS + (ql + 1) * 128],
                                  in_=psa[hb:hb + 64, 0:128]),
                              reads=[psareg], writes=[rattn])
            for oc in range(8):
                so = (oc % 2) * 8 * 128
                rwo = (rs0, rs1)[oc % 2]
                wT_load(P, env, C, ws, wo_ap[oc * 128:(oc + 1) * 128, :], 0, 128,
                        wst, rwo, so)
                for tch in range(HS // 512):
                    pi = tch % 2
                    psu, preg = env.PS[pi], env.rP[pi]
                    def mmo(e, tch=tch, psu=psu, so=so):
                        last = None
                        for dc in range(8):
                            last = e.matmul(psu[:], wst[:, so + dc * 128:so + (dc + 1) * 128],
                                            attn[:, dc * HS + tch * 512:
                                                 dc * HS + (tch + 1) * 512],
                                            start=(dc == 0), stop=(dc == 7))
                        return last
                    P.run("tensor", mmo, reads=[rwo, rattn], writes=[preg])
                    delta_cb(b, oc, half * 2 + tch, psu, preg)


def ceil_div(a, b):
    return -(-a // b)


def emit_spill_x(P, env, C, x_sb, xregs):
    xd = env.dram("x_spill", [NTOK, D], F32)
    xw = x_sb.rearrange("p (i d) -> p i d", d=1024)
    xr = xd.rearrange("(i p) d -> p i d", p=128)
    t = P.run("sync", lambda e: e.dma_start(out=xr, in_=xw), reads=xregs, dma=True)
    env.x_spill = xd
    env.x_spill_tok = t
    return xd, t


def emit_overwrite_with_ln(P, env, C, x_sb, xregs, nm, rstd, rrs):
    for i in range(NT):
        xt = x_sb[:, i * 1024:(i + 1) * 1024]
        P.run("vector",
              lambda e, xt=xt, i=i: e.tensor_scalar(
                  out=xt, in0=xt, scalar1=nm[:, i:i + 1], scalar2=rstd[:, i:i + 1],
                  op0=ALU.add, op1=ALU.mult),
              reads=[rrs, env.x_spill_tok], writes=[xregs[i]])


def emit_gamma_bcast(P, env, C, g_ap, dst, rdst, width=1024, scale_col=None):
    row = env.sbuf("gb_row", [1, 1024], F32)
    rrow = env.reg("gb_row")
    P.run("sync", lambda e: e.dma_start(out=row[0:1, 0:width], in_=g_ap[None, :]),
          writes=[rrow], dma=True)
    for ch in range(width // 512):
        pi = ch % 2
        psu, preg = env.PS[pi], env.rP[pi]
        P.run("tensor",
              lambda e, ch=ch, psu=psu: e.matmul(
                  psu[:], C.ones_f[0:1, 0:128], row[0:1, ch * 512:(ch + 1) * 512],
                  start=True, stop=True),
              reads=[rrow, C.reg], writes=[preg])
        P.run("vector",
              lambda e, ch=ch, psu=psu: e.tensor_copy(
                  out=dst[:, ch * 512:(ch + 1) * 512], in_=psu[:]),
              reads=[preg], writes=[rdst])
    return rdst


def emit_vec_cols(P, env, C, name, g_ap, n=1024):
    buf = env.sbuf(name, [128, n // 128], F32)
    r = env.reg(name)
    P.run("sync",
          lambda e: e.dma_start(out=buf[:], in_=g_ap.rearrange("(c p) -> p c", p=128),
                                allow_slow_non_contiguous=True),
          writes=[r], dma=True)
    return buf, r


def emit_final_add(P, env, C, x_sb, xregs, x_out_ap):
    xd = env.x_spill
    scr = env.scr4k
    xr = xd.rearrange("(i p) d -> p i d", p=128)
    for i in range(NT):
        P.run("sync",
              lambda e, i=i: e.dma_start(out=scr[:], in_=xr[:, i, :]),
              reads=[env.x_spill_tok], writes=[env.rscr], dma=True)
        P.run("vector",
              lambda e, i=i: e.tensor_tensor(
                  out=x_sb[:, i * 1024:(i + 1) * 1024],
                  in0=x_sb[:, i * 1024:(i + 1) * 1024], in1=scr[:], op=ALU.add),
              reads=[env.rscr], writes=[xregs[i]])


def emit_l0_attn(P, env, C, x_sb, xregs, W):
    nm, rstd, rrs = emit_layernorm(P, env, C, x_sb, xregs)
    cb = make_delta_adder(P, env, C, x_sb, xregs)
    wq, wk, wv, wo = W["L0_attn_WQ"], W["L0_attn_WK"], W["L0_attn_WV"], W["L0_attn_WO"]
    getw = lambda kind, g: {"q": wq, "k": wk, "v": wv}[kind][g * 128:(g + 1) * 128, :]
    emit_mha_causal(P, env, C, x_sb, xregs, cb, getw, wo, nm, rstd, rrs)


def emit_l1_attn(P, env, C, x_sb, xregs, W):
    nm, rstd, rrs = emit_layernorm(P, env, C, x_sb, xregs)
    cb = make_delta_adder(P, env, C, x_sb, xregs)
    wq, wk, wv, wo = W["L1_attn_Wq"], W["L1_attn_Wk"], W["L1_attn_Wv"], W["L1_attn_Wo"]
    getw = lambda kind, g: {"q": wq, "k": wk, "v": wv}[kind][g * 128:(g + 1) * 128, :]
    emit_mha_causal(P, env, C, x_sb, xregs, cb, getw, wo, nm, rstd, rrs)


def emit_l2_ffn(P, env, C, x_sb, xregs, W):
    nc = env.nc
    nm, rstd, rrs = emit_layernorm(P, env, C, x_sb, xregs)
    emit_spill_x(P, env, C, x_sb, xregs)
    emit_overwrite_with_ln(P, env, C, x_sb, xregs, nm, rstd, rrs)
    _, rstd1, rrs1 = emit_layernorm(P, env, C, x_sb, xregs, tag="r1", rms=True)
    n1c, rn1c = emit_vec_cols(P, env, C, "n1cols", W["L2_ffn_n1_w"])
    cb = make_delta_adder(P, env, C, x_sb, xregs)
    qkv = W["L2_ffn_qkv_w"]
    getw = lambda kind, g: qkv[{"q": 0, "k": 1024, "v": 2048}[kind] + g * 128:
                               {"q": 0, "k": 1024, "v": 2048}[kind] + (g + 1) * 128, :]
    emit_mha_causal(P, env, C, x_sb, xregs, cb, getw, W["L2_ffn_o_w"],
                    None, rstd1, rrs1, scale=0.125, gamma_cols=n1c, rgc=rn1c,
                    ws_nbuf=1)
    _, rstd2, rrs2 = emit_layernorm(P, env, C, x_sb, xregs, tag="r2", rms=True)
    n2c, rn2c = emit_vec_cols(P, env, C, "n2cols", W["L2_ffn_n2_w"])
    gate8 = env.sbuf("gate8", [128, 80], F32)
    rg8 = env.reg("gate8")
    gs = env.sbuf("gs_cols", [128, NT], F32)
    rgs = env.reg("gs_cols")
    gwsb = env.sb["mha_vt"][:].bitcast(F32)[0:8, :]
    rgw = env.reg("vt")
    P.run("sync", lambda e: e.dma_start(out=gwsb, in_=W["L2_ffn_gate_w"][:]),
          writes=[rgw], dma=True)
    row = env.scr4k
    rrow = env.rscr
    P.run("sync", lambda e: e.dma_start(out=row[0:1, :], in_=W["L2_ffn_n2_w"][None, :]),
          writes=[rrow], dma=True)
    for ch in range(2):
        pi = ch % 2
        psu, preg = env.PS[pi], env.rP[pi]
        P.run("tensor",
              lambda e, ch=ch, psu=psu: e.matmul(
                  psu[0:8, 0:512], C.ones_f[0:1, 0:8], row[0:1, ch * 512:(ch + 1) * 512],
                  start=True, stop=True),
              reads=[rrow, C.reg], writes=[preg])
        P.run("vector",
              lambda e, ch=ch, psu=psu: e.tensor_tensor(
                  out=gwsb[0:8, ch * 512:(ch + 1) * 512],
                  in0=gwsb[0:8, ch * 512:(ch + 1) * 512], in1=psu[0:8, 0:512], op=ALU.mult),
              reads=[preg], writes=[rgw])
    gnT = env.sbuf("gnT", [128, 64], F32)
    rgnT = env.reg("gnT")
    for dc in range(8):
        tb, tbr = (env.P4, env.rP[4]) if dc % 2 == 0 else (env.P5, env.rP[5])
        P.run("tensor",
              lambda e, dc=dc, tb=tb: e.transpose(
                  tb[0:128, 0:8], gwsb[0:8, dc * 128:(dc + 1) * 128], C.ident_f[0:8, 0:8]),
              reads=[rgw, C.reg], writes=[tbr])
        P.run("vector",
              lambda e, dc=dc, tb=tb: e.tensor_copy(out=gnT[:, dc * 8:(dc + 1) * 8],
                                                    in_=tb[0:128, 0:8]),
              reads=[tbr], writes=[rgnT])
    for i in range(NT):
        for dc in range(8):
            tb, tbr = (env.P4, env.rP[4]) if dc % 2 == 0 else (env.P5, env.rP[5])
            P.run("tensor",
                  lambda e, i=i, dc=dc, tb=tb: e.transpose(
                      tb[:, 0:128], x_sb[:, i * 1024 + dc * 128:i * 1024 + (dc + 1) * 128],
                      C.ident_f),
                  reads=[xregs[i], C.reg], writes=[tbr])
            P.run("vector",
                  lambda e, dc=dc, tb=tb: e.tensor_copy(
                      out=env.scr4k[:, dc * 128:(dc + 1) * 128], in_=tb[:, 0:128]),
                  reads=[tbr], writes=[env.rscr])
        def mml(e, i=i):
            last = None
            for dc in range(8):
                last = e.matmul(env.P3[:, 0:8], env.scr4k[:, dc * 128:(dc + 1) * 128],
                                gnT[:, dc * 8:(dc + 1) * 8],
                                start=(dc == 0), stop=(dc == 7))
            return last
        P.run("tensor", mml, reads=[env.rscr, rgnT], writes=[env.rP[3]])
        P.run("vector",
              lambda e, i=i: e.tensor_scalar(out=gate8[:, 0:8], in0=env.P3[:, 0:8],
                                             scalar1=rstd2[:, i:i + 1], scalar2=None,
                                             op0=ALU.mult),
              reads=[env.rP[3], rrs2], writes=[rg8])
        P.run("vector",
              lambda e: e.reduce_max(out=gate8[:, 8:9], in_=gate8[:, 0:8], axis=AX.X,
                                     negate=True),
              writes=[rg8])
        P.run("scalar",
              lambda e: e.activation(out=gate8[:, 16:24], in_=gate8[:, 0:8], func=AF.Exp,
                                     bias=gate8[:, 8:9], accum_out=gate8[:, 9:10]),
              writes=[rg8])
        P.run("vector", lambda e: e.max(out=gate8[:, 24:32], in_=gate8[:, 16:24]),
              writes=[rg8])
        P.run("vector",
              lambda e: e.tensor_tensor(out=gate8[:, 10:11], in0=gate8[:, 24:25],
                                        in1=gate8[:, 25:26], op=ALU.add),
              writes=[rg8])
        P.run("vector",
              lambda e: e.scalar_tensor_tensor(out=gate8[:, 11:12], in0=gate8[:, 9:10],
                                               scalar=1e-9, in1=gate8[:, 10:11],
                                               op0=ALU.mult, op1=ALU.add),
              writes=[rg8])
        P.run("vector", lambda e: e.reciprocal(out=gate8[:, 12:13], in_=gate8[:, 11:12]),
              writes=[rg8])
        P.run("vector",
              lambda e, i=i: e.tensor_tensor(out=gs[:, i:i + 1], in0=gate8[:, 10:11],
                                             in1=gate8[:, 12:13], op=ALU.mult),
              reads=[rg8], writes=[rgs])
    h_dram = env.dram("l2ffn_h", [B_LOC, 4096, S], BF16)
    rhdram = env.reg("l2ffn_hdram")
    xn_fm = env.sb["xn_fm"]
    rxn = env.reg("xn_fm")
    wst = env.sb["mha_wst"]
    rs0 = env.reg("wslot0")
    ws = WSlice(P, env, "mha", 1024, 1)
    ws = WSlice(P, env, "mha", 1024, 1)
    hst = env.sb["mha_probs"]
    rhst = env.reg("probs")
    up, down = W["L2_ffn_up_w"], W["L2_ffn_down_w"]
    for b in range(B_LOC):
        emit_xnfm(P, env, C, x_sb, xregs, None, rstd2, rrs2, b, xn_fm, rxn)
        for fc in range(32):
            wT_load(P, env, C, ws, up[fc * 128:(fc + 1) * 128, :], 0, 128, wst, rs0, 0,
                    gamma_cols=n2c, rgc=rn2c)
            for tch in range(S // 512):
                pi = tch % 2
                psu, preg = env.PS[pi], env.rP[pi]
                def mmh(e, fc=fc, tch=tch, psu=psu):
                    last = None
                    for dc in range(8):
                        last = e.matmul(psu[:], wst[:, dc * 128:(dc + 1) * 128],
                                        xn_fm[:, dc * S + tch * 512:dc * S + (tch + 1) * 512],
                                        start=(dc == 0), stop=(dc == 7))
                    return last
                P.run("tensor", mmh, reads=[rs0, rxn], writes=[preg])
                P.run("scalar",
                      lambda e, tch=tch, psu=psu: e.activation(
                          out=hst[:, tch * 512:(tch + 1) * 512], in_=psu[:], func=AF.Silu),
                      reads=[preg], writes=[rhst])
            P.run("sync",
                  lambda e, b=b, fc=fc: e.dma_start(
                      out=h_dram[b, fc * 128:(fc + 1) * 128, :], in_=hst[:]),
                  reads=[rhst], writes=[rhdram], dma=True)
        hld = env.sb["mha_kf"]
        hld2 = env.sb["mha_attn"]
        rhld = [env.reg("kf"), env.reg("attn")]
        hlds = [hld, hld2]
        dwt = env.sb["mha_qf"]
        rdwt = [env.reg("qf"), env.reg("qf2")]
        for oc in range(8):
            for fc in range(32):
                pi = fc % 2
                view, vreg = ws.load(down[oc * 128:(oc + 1) * 128,
                                          fc * 128:(fc + 1) * 128])
                transpose128(P, env, C, dwt[:, pi * 128:pi * 128 + 128],
                             view[0:128, 0:128], [vreg], rdwt[pi], ti=pi)
                P.run("sync",
                      lambda e, b=b, fc=fc, pi=pi: e.dma_start(
                          out=hlds[pi][:, 0:S],
                          in_=h_dram[b, fc * 128:(fc + 1) * 128, :]),
                      reads=[rhdram], writes=[rhld[pi]], dma=True)
                for tch in range(S // 512):
                    psu, preg = env.PS[tch], env.rP[tch]
                    P.run("tensor",
                          lambda e, tch=tch, fc=fc, psu=psu, pi=pi: e.matmul(
                              psu[:], dwt[:, pi * 128:pi * 128 + 128],
                              hlds[pi][:, tch * 512:(tch + 1) * 512],
                              start=(fc == 0), stop=(fc == 31)),
                          reads=[rdwt[pi], rhld[pi]], writes=[preg])
            for tch in range(S // 512):
                psu, preg = env.PS[tch], env.rP[tch]
                P.run("vector",
                      lambda e, tch=tch, psu=psu: e.tensor_copy(
                          out=env.scr4k[:, 0:512], in_=psu[:]),
                      reads=[preg], writes=[env.rscr])
                for j in range(4):
                    it = b * NT_B + (tch * 512 + j * 128) // 128
                    tb, tbr = (env.P4, env.rP[4]) if j % 2 == 0 else (env.P5, env.rP[5])
                    P.run("tensor",
                          lambda e, j=j, tb=tb: e.transpose(
                              tb[:, 0:128], env.scr4k[:, j * 128:(j + 1) * 128], C.ident_f),
                          reads=[env.rscr, C.reg], writes=[tbr])
                    P.run("vector",
                          lambda e, it=it, oc=oc, tb=tb: e.scalar_tensor_tensor(
                              out=x_sb[:, it * 1024 + oc * 128:it * 1024 + (oc + 1) * 128],
                              in0=tb[:, 0:128], scalar=gs[:, it:it + 1],
                              in1=x_sb[:, it * 1024 + oc * 128:it * 1024 + (oc + 1) * 128],
                              op0=ALU.mult, op1=ALU.add),
                          reads=[tbr, rgs], writes=[xregs[it]])
    env.needs_final_add = True



def emit_l3_ffn(P, env, C, x_sb, xregs, W):
    nc = env.nc
    nm, rstd, rrs = emit_layernorm(P, env, C, x_sb, xregs)
    emit_spill_x(P, env, C, x_sb, xregs)
    emit_overwrite_with_ln(P, env, C, x_sb, xregs, nm, rstd, rrs)
    env.needs_final_add = True
    SUP = 512
    NSUP = NTOK // SUP

    qw, sc_ap = W["L3_ffn_qweights"], W["L3_ffn_scales"]
    bia, la_ap, lb_ap = W["L3_ffn_biases"], W["L3_ffn_lora_a"], W["L3_ffn_lora_b"]
    lng, lnb = W["L3_ffn_ln_gamma"], W["L3_ffn_ln_beta"]

    xn_fm = env.sbuf("l3_fm", [128, 8192], BF16)
    rhfm, ryfm = env.reg("hfm"), env.reg("yfm")
    dq = env.sbuf("l3_dq", [128, 5120], BF16)
    rdq = env.reg("l3_dq")
    qstage = dq[:].bitcast(I32)[:, 0:1024]
    tmpf = dq[:].bitcast(F32)[:, 1024:2048]
    wrow = dq[:, 4096:5120]
    scstage = env.sbuf("l3_scst", [128, 64], F32)
    rscst = env.reg("l3_scst")
    y0 = env.sbuf("l3_y0", [128, 4 * 1024], F32)
    ry0 = env.reg("l3_y0")
    wst = env.sbuf("mha_wst", [128, 2 * 8 * 128], BF16)
    rwt = env.reg("wslot0")
    lat = env.sbuf("l3_lat", [128, 8 * 32], BF16)
    lbt = env.sbuf("l3_lbt", [32, 8 * 128], BF16)
    rlat, rlbt = env.reg("l3_lat"), env.reg("l3_lbt")
    lst = env.sbuf("mha_qf", [128, S // 2], BF16)
    rlst = env.reg("qf")
    xr_sb = env.sbuf("mha_probs", [128, S], BF16)
    rxr = env.reg("probs")
    gam = env.sbuf("mha_kf", [128, S], BF16)[:].bitcast(F32)
    rgamb = env.reg("kf")
    bet = env.sbuf("mha_vt", [128, S], BF16)[:].bitcast(F32)
    rbetb = env.reg("vt")
    bcols = env.sbuf("l3_bcols", [128, 16], F32)
    rbc = env.reg("l3_bcols")
    lncol = env.sbuf("l3_ln", [128, 8], F32)
    rln = env.reg("l3_ln")
    scrb = env.scr4k[:].bitcast(BF16)

    def dequant_wT(i, oc):
        P.run("gpsimd",
              lambda e: e.dma_start(out=qstage, in_=qw[i, oc * 128:(oc + 1) * 128, :]),
              writes=[rdq], dma=True)
        P.run("sync",
              lambda e: e.dma_start(out=scstage[:], in_=sc_ap[i, oc * 128:(oc + 1) * 128, :]),
              writes=[rscst], dma=True)
        P.run("vector",
              lambda e: e.tensor_scalar(out=tmpf, in0=qstage,
                                        scalar1=float(1.0 / 7.5), scalar2=-1.0,
                                        op0=ALU.mult, op1=ALU.add),
              writes=[rdq])
        P.run("vector",
              lambda e: e.tensor_tensor(
                  out=wrow.rearrange("p (g r) -> p g r", r=16),
                  in0=tmpf.rearrange("p (g r) -> p g r", r=16),
                  in1=scstage[:, :, None].broadcast_to([128, 64, 16]),
                  op=ALU.mult),
              reads=[rscst], writes=[rdq])
        bank, breg = env.TS[0], env.rT[0]
        for dc in range(8):
            P.run("tensor",
                  lambda e, dc=dc, bank=bank: e.transpose(
                      bank[:, dc * 128:(dc + 1) * 128],
                      wrow[:, dc * 128:(dc + 1) * 128], C.ident_b),
                  reads=[rdq, C.reg], writes=[breg])
        P.run("vector",
              lambda e, bank=bank: e.tensor_copy(out=wst[:, 0:1024], in_=bank[:, 0:1024]),
              reads=[breg], writes=[rwt])

    def load_lora(i):
        P.run("gpsimd", lambda e: e.dma_start(out=lst[0:32, :], in_=la_ap[i]),
              writes=[rlst], dma=True)
        for dc in range(8):
            transpose128(P, env, C, lat[:, dc * 32:(dc + 1) * 32],
                         lst[0:32, dc * 128:(dc + 1) * 128], [rlst], rlat, ti=dc % 2)
        P.run("gpsimd",
              lambda e: e.dma_start(out=lst[:, 0:256].rearrange("p (c r) -> p c r", r=32),
                                    in_=lb_ap[i].rearrange("(c p) r -> p c r", p=128)),
              writes=[rlst], dma=True)
        for oc in range(8):
            transpose128(P, env, C, lbt[0:32, oc * 128:(oc + 1) * 128],
                         lst[:, oc * 32:(oc + 1) * 32], [rlst], rlbt, ti=oc % 2)

    def qlora_pass(i, src_fm, rsrc, out_cb, skip_lora_load):
        if not skip_lora_load:
            load_lora(i)
        def mmxr(e):
            last = None
            for dc in range(8):
                last = e.matmul(env.P2[0:32, 0:512], lat[:, dc * 32:(dc + 1) * 32],
                                src_fm[:, dc * SUP:dc * SUP + 512],
                                start=(dc == 0), stop=(dc == 7))
            return last
        P.run("tensor", mmxr, reads=[rlat, rsrc], writes=[env.rP[2]])
        P.run("scalar", lambda e: e.copy(out=xr_sb[0:32, 0:512], in_=env.P2[0:32, 0:512]),
              reads=[env.rP[2]], writes=[rxr])
        for oc in range(8):
            dequant_wT(i, oc)
            psu, preg = env.PS[oc % 2], env.rP[oc % 2]
            def mmw(e, psu=psu):
                last = None
                for dc in range(8):
                    last = e.matmul(psu[:], wst[:, dc * 128:(dc + 1) * 128],
                                    src_fm[:, dc * SUP:dc * SUP + 512],
                                    start=(dc == 0), stop=False)
                return last
            P.run("tensor", mmw, reads=[rwt, rsrc], writes=[preg])
            P.run("tensor",
                  lambda e, oc=oc, psu=psu: e.matmul(
                      psu[:], lbt[0:32, oc * 128:(oc + 1) * 128],
                      xr_sb[0:32, 0:512], start=False, stop=True),
                  reads=[rlbt, rxr], writes=[preg])
            out_cb(oc, psu, preg)

    for blk in range(3):
        i0, i1 = 2 * blk, 2 * blk + 1
        P.run("sync",
              lambda e, i0=i0: e.dma_start(out=bcols[:, 0:8],
                                           in_=bia[i0].rearrange("(c p) -> p c", p=128),
                                           allow_slow_non_contiguous=True),
              writes=[rbc], dma=True)
        P.run("sync",
              lambda e, i1=i1: e.dma_start(out=bcols[:, 8:16],
                                           in_=bia[i1].rearrange("(c p) -> p c", p=128),
                                           allow_slow_non_contiguous=True),
              writes=[rbc], dma=True)
        emit_gamma_bcast(P, env, C, lng[blk], gam, rgamb)
        emit_gamma_bcast(P, env, C, lnb[blk], bet, rbetb)
        for sup in range(NSUP):
            for tl in range(4):
                it = sup * 4 + tl
                P.run("vector",
                      lambda e, it=it: e.tensor_copy(
                          out=scrb[:, 0:1024], in_=x_sb[:, it * 1024:(it + 1) * 1024]),
                      reads=[xregs[it]], writes=[env.rscr])
                bank, breg = env.TS[tl % 2], env.rT[tl % 2]
                for dc in range(8):
                    P.run("tensor",
                          lambda e, dc=dc, bank=bank: e.transpose(
                              bank[:, dc * 128:(dc + 1) * 128],
                              scrb[:, dc * 128:(dc + 1) * 128], C.ident_b),
                          reads=[env.rscr, C.reg], writes=[breg])
                P.run("vector",
                      lambda e, tl=tl, bank=bank: e.tensor_copy(
                          out=xn_fm[:, 0:8 * SUP].rearrange("p (c t) -> p c t", t=SUP)
                              [:, :, tl * 128:(tl + 1) * 128],
                          in_=bank[:, 0:1024].rearrange("p (c w) -> p c w", w=128)),
                      reads=[breg], writes=[rhfm])
            def out0(oc, psu, preg):
                P.run("vector",
                      lambda e, oc=oc, psu=psu: e.tensor_scalar(
                          out=env.scr4k[:, 0:512], in0=psu[:],
                          scalar1=bcols[:, oc:oc + 1], scalar2=None, op0=ALU.add),
                      reads=[preg, rbc], writes=[env.rscr])
                for j in range(4):
                    tb, tbr = (env.P4, env.rP[4]) if j % 2 == 0 else (env.P5, env.rP[5])
                    P.run("tensor",
                          lambda e, j=j, tb=tb: e.transpose(
                              tb[:, 0:128], env.scr4k[:, j * 128:(j + 1) * 128], C.ident_f),
                          reads=[env.rscr, C.reg], writes=[tbr])
                    P.run("vector",
                          lambda e, j=j, oc=oc, tb=tb: e.tensor_copy(
                              out=y0[:, j * 1024 + oc * 128:j * 1024 + (oc + 1) * 128],
                              in_=tb[:, 0:128]),
                          reads=[tbr], writes=[ry0])
            qlora_pass(i0, xn_fm[:, 0:8 * SUP], rhfm, out0, skip_lora_load=(sup > 0))
            for tl in range(4):
                ysl = y0[:, tl * 1024:(tl + 1) * 1024]
                P.run("vector",
                      lambda e, ysl=ysl: e.reduce_sum(out=lncol[:, 0:1], in_=ysl, axis=AX.X),
                      reads=[ry0], writes=[rln])
                P.run("scalar",
                      lambda e, ysl=ysl: e.activation(out=env.scr4k[:], in_=ysl,
                                                      func=AF.Square,
                                                      accum_out=lncol[:, 1:2]),
                      reads=[ry0], writes=[env.rscr, rln])
                P.run("vector",
                      lambda e: e.tensor_scalar(out=lncol[:, 2:3], in0=lncol[:, 0:1],
                                                scalar1=float(-1.0 / 1024.0), scalar2=None,
                                                op0=ALU.mult),
                      writes=[rln])
                P.run("vector",
                      lambda e: e.scalar_tensor_tensor(out=lncol[:, 3:4], in0=lncol[:, 2:3],
                                                       scalar=-1.0, in1=lncol[:, 2:3],
                                                       op0=ALU.mult, op1=ALU.mult),
                      writes=[rln])
                P.run("vector",
                      lambda e: e.scalar_tensor_tensor(out=lncol[:, 4:5], in0=lncol[:, 1:2],
                                                       scalar=float(1.0 / 1024.0),
                                                       in1=lncol[:, 3:4],
                                                       op0=ALU.mult, op1=ALU.add),
                      writes=[rln])
                P.run("scalar",
                      lambda e: e.activation(out=lncol[:, 5:6], in_=lncol[:, 4:5],
                                             func=AF.Sqrt, bias=C.col(1e-5)),
                      reads=[C.reg], writes=[rln])
                P.run("vector", lambda e: e.reciprocal(out=lncol[:, 6:7], in_=lncol[:, 5:6]),
                      writes=[rln])
                P.run("vector",
                      lambda e, ysl=ysl: e.tensor_scalar(
                          out=env.scr4k[:], in0=ysl, scalar1=lncol[:, 2:3],
                          scalar2=lncol[:, 6:7], op0=ALU.add, op1=ALU.mult),
                      reads=[ry0, rln], writes=[env.rscr])
                P.run("vector",
                      lambda e: e.tensor_tensor(out=env.scr4k[:], in0=env.scr4k[:],
                                                in1=gam[:, 0:1024], op=ALU.mult),
                      reads=[rgamb], writes=[env.rscr])
                P.run("vector",
                      lambda e: e.tensor_tensor(out=env.scr4k[:], in0=env.scr4k[:],
                                                in1=bet[:, 0:1024], op=ALU.add),
                      reads=[rbetb], writes=[env.rscr])
                P.run("scalar",
                      lambda e, ysl=ysl: e.activation(out=ysl, in_=env.scr4k[:], func=AF.Gelu),
                      reads=[env.rscr], writes=[ry0])
                P.run("vector",
                      lambda e, ysl=ysl: e.tensor_copy(out=scrb[:, 0:1024], in_=ysl),
                      reads=[ry0], writes=[env.rscr])
                bank, breg = env.TS[tl % 2], env.rT[tl % 2]
                for dc in range(8):
                    P.run("tensor",
                          lambda e, dc=dc, bank=bank: e.transpose(
                              bank[:, dc * 128:(dc + 1) * 128],
                              scrb[:, dc * 128:(dc + 1) * 128], C.ident_b),
                          reads=[env.rscr, C.reg], writes=[breg])
                P.run("vector",
                      lambda e, tl=tl, bank=bank: e.tensor_copy(
                          out=xn_fm[:, 4096:4096 + 8 * SUP]
                              .rearrange("p (c t) -> p c t", t=SUP)
                              [:, :, tl * 128:(tl + 1) * 128],
                          in_=bank[:, 0:1024].rearrange("p (c w) -> p c w", w=128)),
                      reads=[breg], writes=[ryfm])
            def out1(oc, psu, preg, sup=sup):
                P.run("vector",
                      lambda e, oc=oc, psu=psu: e.tensor_scalar(
                          out=env.scr4k[:, 0:512], in0=psu[:],
                          scalar1=bcols[:, 8 + oc:9 + oc], scalar2=None, op0=ALU.add),
                      reads=[preg, rbc], writes=[env.rscr])
                for j in range(4):
                    tb, tbr = (env.P4, env.rP[4]) if j % 2 == 0 else (env.P5, env.rP[5])
                    P.run("tensor",
                          lambda e, j=j, tb=tb: e.transpose(
                              tb[:, 0:128], env.scr4k[:, j * 128:(j + 1) * 128], C.ident_f),
                          reads=[env.rscr, C.reg], writes=[tbr])
                    it = sup * 4 + j
                    P.run("vector",
                          lambda e, it=it, oc=oc, tb=tb: e.tensor_tensor(
                              out=x_sb[:, it * 1024 + oc * 128:it * 1024 + (oc + 1) * 128],
                              in0=x_sb[:, it * 1024 + oc * 128:it * 1024 + (oc + 1) * 128],
                              in1=tb[:, 0:128], op=ALU.add),
                          reads=[tbr], writes=[xregs[it]])
            qlora_pass(i1, xn_fm[:, 4096:8192], ryfm, out1, skip_lora_load=(sup > 0))


def emit_hilo_router(P, env, C, x_sb, xregs, wr_ap,
                     g2c, rg2c, gw_sb, rgw, zhl, rzhl, finish_cb, load_gw=True,
                     wr_dmajor=False):
    wrT = env.sbuf("rt_wrT", [128, 8 * 8], F32)
    rwrT = env.reg("rt_wrT")
    if load_gw and not wr_dmajor:
        P.run("sync", lambda e: e.dma_start(out=gw_sb, in_=wr_ap[:]), writes=[rgw], dma=True)
    wrB = env.sbuf("rt_wrB", [128, 2 * 8 * 8], BF16)
    rwrB = env.reg("rt_wrB")
    if wr_dmajor:
        P.run("sync",
              lambda e: e.dma_start(out=wrT[:].rearrange("p (c e) -> p c e", e=8),
                                    in_=wr_ap.rearrange("(c p) e -> p c e", p=128)),
              writes=[rwrT], dma=True)
    for dc in range(8 if not wr_dmajor else 0):
        tb, tbr = (env.P4, env.rP[4]) if dc % 2 == 0 else (env.P5, env.rP[5])
        P.run("tensor",
              lambda e, dc=dc, tb=tb: e.transpose(
                  tb[0:128, 0:8], gw_sb[0:8, dc * 128:(dc + 1) * 128], C.ident_f[0:8, 0:8]),
              reads=[rgw, C.reg], writes=[tbr])
        if g2c is not None:
            P.run("vector",
                  lambda e, dc=dc, tb=tb: e.tensor_scalar(
                      out=wrT[:, dc * 8:(dc + 1) * 8], in0=tb[0:128, 0:8],
                      scalar1=g2c[:, dc:dc + 1], scalar2=None, op0=ALU.mult),
                  reads=[tbr, rg2c], writes=[rwrT])
        else:
            P.run("vector",
                  lambda e, dc=dc, tb=tb: e.tensor_copy(
                      out=wrT[:, dc * 8:(dc + 1) * 8], in_=tb[0:128, 0:8]),
                  reads=[tbr], writes=[rwrT])
    P.run("vector", lambda e: e.tensor_copy(out=wrB[:, 0:64], in_=wrT[:]),
          reads=[rwrT], writes=[rwrB])
    P.run("vector",
          lambda e: e.tensor_tensor(out=wrB[:, 64:128], in0=wrT[:],
                                    in1=wrB[:, 0:64], op=ALU.subtract),
          reads=[rwrT], writes=[rwrB])
    for i in range(NT):
        for dc in range(8):
            tb, tbr = (env.P4, env.rP[4]) if dc % 2 == 0 else (env.P5, env.rP[5])
            P.run("tensor",
                  lambda e, i=i, dc=dc, tb=tb: e.transpose(
                      tb[:, 0:128], x_sb[:, i * 1024 + dc * 128:i * 1024 + (dc + 1) * 128],
                      C.ident_f),
                  reads=[xregs[i], C.reg], writes=[tbr])
            P.run("vector",
                  lambda e, dc=dc, tb=tb: e.tensor_copy(
                      out=env.scr4k[:, dc * 128:(dc + 1) * 128], in_=tb[:, 0:128]),
                  reads=[tbr], writes=[env.rscr])
        P.run("vector", lambda e: e.tensor_copy(out=zhl[:, 0:1024], in_=env.scr4k[:]),
              reads=[env.rscr], writes=[rzhl])
        P.run("vector",
              lambda e: e.tensor_tensor(out=zhl[:, 1024:2048], in0=env.scr4k[:],
                                        in1=zhl[:, 0:1024], op=ALU.subtract),
              reads=[env.rscr], writes=[rzhl])
        def mml2(e, i=i):
            last = None
            n = 0
            for dc in range(8):
                for (zo, wo) in ((0, 0), (0, 64), (1024, 0)):
                    n += 1
                    last = e.matmul(env.P3[:, 0:8],
                                    zhl[:, zo + dc * 128:zo + (dc + 1) * 128],
                                    wrB[:, wo + dc * 8:wo + (dc + 1) * 8],
                                    start=(n == 1), stop=(n == 24))
            return last
        P.run("tensor", mml2, reads=[rzhl, rwrB, C.reg], writes=[env.rP[3]])
        finish_cb(i, env.P3[:, 0:8], env.rP[3])


def emit_l1_ffn(P, env, C, x_sb, xregs, W):
    nc = env.nc
    H, DK = 4, 256
    nm, rstd, rrs = emit_layernorm(P, env, C, x_sb, xregs)
    emit_spill_x(P, env, C, x_sb, xregs)
    emit_overwrite_with_ln(P, env, C, x_sb, xregs, nm, rstd, rrs)
    env.needs_final_add = True
    _, rstd1, rrs1 = emit_layernorm(P, env, C, x_sb, xregs, tag="r1", rms=True)
    g1c, rg1c = emit_vec_cols(P, env, C, "g1cols", W["L1_ffn_g1"])

    xn_fm = env.sbuf("xn_fm", [128, 8 * S], BF16)
    rxn = env.reg("xn_fm")
    kcf = env.sbuf("mla_kc", [128, 2 * S], BF16)
    vct = env.sbuf("mla_vc", [128, 16 * 256], BF16)
    rkc, rvc = env.reg("mla_kc"), env.reg("mla_vc")
    qfh = env.sbuf("mla_q", [128, 2 * 512], BF16)
    rqf = env.reg("mla_q")
    aout = env.sbuf("mla_ao", [128, 8 * 512], BF16)
    raout = env.reg("mla_ao")
    probs = env.sbuf("mha_probs", [128, S], BF16)
    rprobs = env.reg("probs")
    pT = env.sbuf("mha_pT", [128, 512], BF16)
    rpT = env.reg("pT")
    ncols = env.sbuf("mha_ncols", [128, 16], F32)
    rncols = env.reg("ncols")
    dcc = env.sbuf("mha_dc", [128, 8], F32)
    rdcc = env.reg("dcc")
    wst = env.sbuf("mha_wst", [128, 2 * 8 * 128], BF16)
    rwt = env.reg("wslot0")
    rwt3 = env.reg("wslot1")
    ws = WSlice(P, env, "mha", 1024, 1)
    scale = 0.0625

    cb = make_delta_adder(P, env, C, x_sb, xregs)
    for b in range(B_LOC):
        emit_xnfm(P, env, C, x_sb, xregs, None, rstd1, rrs1, b, xn_fm, rxn)
        wT_load(P, env, C, ws, W["L1_ffn_Wkc"], 0, 256, wst, rwt, 0,
                gamma_cols=g1c, rgc=rg1c)
        for oc2 in range(2):
            for tch in range(S // 512):
                pi = tch % 2
                psu, preg = env.PS[pi], env.rP[pi]
                def mmk(e, oc2=oc2, tch=tch, psu=psu):
                    last = None
                    for dc in range(8):
                        last = e.matmul(psu[:], wst[:, dc * 256 + oc2 * 128:
                                                    dc * 256 + oc2 * 128 + 128],
                                        xn_fm[:, dc * S + tch * 512:dc * S + (tch + 1) * 512],
                                        start=(dc == 0), stop=(dc == 7))
                    return last
                P.run("tensor", mmk, reads=[rwt, rxn], writes=[preg])
                P.run("scalar",
                      lambda e, oc2=oc2, tch=tch, psu=psu: e.copy(
                          out=kcf[:, oc2 * S + tch * 512:oc2 * S + (tch + 1) * 512],
                          in_=psu[:]),
                      reads=[preg], writes=[rkc])
        wT_load(P, env, C, ws, W["L1_ffn_Wvc"], 0, 256, wst, rwt, 0,
                gamma_cols=g1c, rgc=rg1c)
        for it in range(NT_B):
            pi = it % 2
            psu, preg = env.PS[2 + pi], env.rP[2 + pi]
            def mmv(e, it=it, psu=psu):
                last = None
                for dc in range(8):
                    last = e.matmul(psu[:, 0:256],
                                    xn_fm[:, dc * S + it * 128:dc * S + (it + 1) * 128],
                                    wst[:, dc * 256:(dc + 1) * 256],
                                    start=(dc == 0), stop=(dc == 7))
                return last
            P.run("tensor", mmv, reads=[rwt, rxn], writes=[preg])
            P.run("scalar",
                  lambda e, it=it, psu=psu: e.copy(
                      out=vct[:, it * 256:(it + 1) * 256], in_=psu[:, 0:256]),
                  reads=[preg], writes=[rvc])
        P.run("scalar",
              lambda e: e.activation(out=probs[:, 0:S], in_=kcf[:, 0:S], func=AF.Square),
              reads=[rkc], writes=[rprobs])
        for it in range(NT_B):
            P.run("tensor",
                  lambda e, it=it: e.matmul(
                      env.P2[:, it:it + 1], probs[:, it * 128:(it + 1) * 128],
                      C.ones_b[:, 0:1], start=True, stop=True),
                  reads=[rprobs, C.reg], writes=[env.rP[2]])
        P.run("vector",
              lambda e: e.tensor_copy(out=env.scr4k[:, 0:NT_B], in_=env.P2[:, 0:NT_B]),
              reads=[env.rP[2]], writes=[env.rscr])
        P.run("scalar",
              lambda e: e.activation(out=probs[:, 0:S], in_=kcf[:, S:2 * S], func=AF.Square),
              reads=[rkc], writes=[rprobs])
        for it in range(NT_B):
            P.run("tensor",
                  lambda e, it=it: e.matmul(
                      env.P2[:, it:it + 1], probs[:, it * 128:(it + 1) * 128],
                      C.ones_b[:, 0:1], start=True, stop=True),
                  reads=[rprobs, C.reg], writes=[env.rP[2]])
        P.run("vector",
              lambda e: e.tensor_tensor(out=env.scr4k[:, 0:NT_B], in0=env.scr4k[:, 0:NT_B],
                                        in1=env.P2[:, 0:NT_B], op=ALU.add),
              reads=[env.rP[2]], writes=[env.rscr])
        P.run("vector",
              lambda e: e.reduce_max(out=ncols[:, 8:9], in_=env.scr4k[:, 0:NT_B], axis=AX.X),
              reads=[env.rscr], writes=[rncols])
        P.run("tensor",
              lambda e: e.transpose(env.P4[0:1, 0:128], ncols[:, 8:9], C.ident_f),
              reads=[rncols, C.reg], writes=[env.rP[4]])
        P.run("vector",
              lambda e: e.reduce_max(out=ncols[0:1, 9:10], in_=env.P4[0:1, 0:128],
                                     axis=AX.X),
              reads=[env.rP[4]], writes=[rncols])
        P.run("scalar",
              lambda e: e.activation(out=ncols[0:1, 10:11], in_=ncols[0:1, 9:10],
                                     func=AF.Sqrt),
              writes=[rncols])
        P.run("tensor",
              lambda e: e.matmul(env.P3[:, 0:1], C.ones_f[0:1, 0:128],
                                 ncols[0:1, 10:11], start=True, stop=True),
              reads=[rncols, C.reg], writes=[env.rP[3]])
        P.run("vector",
              lambda e: e.tensor_scalar(out=ncols[:, 11:12], in0=env.P3[:, 0:1],
                                        scalar1=float(-scale), scalar2=None, op0=ALU.mult),
              reads=[env.rP[3]], writes=[rncols])
        for st4 in range(4):
            q0 = st4 * 512
            for h in range(H):
                wT_load(P, env, C, ws, W["L1_ffn_Wq"], h * DK, 256, wst, rwt, 0,
                        gamma_cols=g1c, rgc=rg1c)
                for oc2 in range(2):
                    psu, preg = env.PS[oc2 % 2], env.rP[oc2 % 2]
                    def mmq(e, oc2=oc2, psu=psu, q0=q0):
                        last = None
                        for dc in range(8):
                            last = e.matmul(psu[:], wst[:, dc * 256 + oc2 * 128:
                                                        dc * 256 + oc2 * 128 + 128],
                                            xn_fm[:, dc * S + q0:dc * S + q0 + 512],
                                            start=(dc == 0), stop=(dc == 7))
                        return last
                    P.run("tensor", mmq, reads=[rwt, rxn], writes=[preg])
                    P.run("scalar",
                          lambda e, oc2=oc2, psu=psu: e.copy(
                              out=qfh[:, oc2 * 512:(oc2 + 1) * 512], in_=psu[:]),
                          reads=[preg], writes=[rqf])
                P.run("scalar",
                      lambda e: e.activation(out=probs[:, 0:1024], in_=qfh[:, 0:1024],
                                             func=AF.Square),
                      reads=[rqf], writes=[rprobs])
                for qt in range(4):
                    def mmn2(e, qt=qt):
                        e.matmul(env.P2[:, qt:qt + 1],
                                 probs[:, qt * 128:(qt + 1) * 128], C.ones_b[:, 0:1],
                                 start=True, stop=False)
                        return e.matmul(env.P2[:, qt:qt + 1],
                                        probs[:, 512 + qt * 128:512 + (qt + 1) * 128],
                                        C.ones_b[:, 0:1], start=False, stop=True)
                    P.run("tensor", mmn2, reads=[rprobs, C.reg], writes=[env.rP[2]])
                P.run("scalar",
                      lambda e: e.activation(out=ncols[:, 0:4], in_=env.P2[:, 0:4],
                                             func=AF.Sqrt),
                      reads=[env.rP[2]], writes=[rncols])
                P.run("vector",
                      lambda e: e.tensor_scalar(out=ncols[:, 0:4], in0=ncols[:, 0:4],
                                                scalar1=ncols[:, 11:12], scalar2=-0.1,
                                                op0=ALU.mult, op1=ALU.add),
                      writes=[rncols])
                for qt in range(4):
                    for kch in range(4):
                        pi = kch % 2
                        psu, preg = env.PS[pi], env.rP[pi]
                        def mms(e, qt=qt, kch=kch, psu=psu):
                            last = None
                            for oc2 in range(2):
                                last = e.matmul(
                                    psu[:],
                                    qfh[:, oc2 * 512 + qt * 128:oc2 * 512 + (qt + 1) * 128],
                                    kcf[:, oc2 * S + kch * 512:oc2 * S + (kch + 1) * 512],
                                    start=(oc2 == 0), stop=(oc2 == 1))
                            return last
                        P.run("tensor", mms, reads=[rqf, rkc], writes=[preg])
                        P.run("scalar",
                              lambda e, kch=kch, qt=qt, psu=psu: e.activation(
                                  out=probs[:, kch * 512:(kch + 1) * 512], in_=psu[:],
                                  func=AF.Exp, scale=float(scale), bias=ncols[:, qt:qt + 1],
                                  accum_out=dcc[:, kch:kch + 1]),
                              reads=[preg, rncols], writes=[rprobs, rdcc])
                    P.run("vector",
                          lambda e: e.reduce_sum(out=dcc[:, 4:5], in_=dcc[:, 0:4], axis=AX.X),
                          writes=[rdcc])
                    P.run("vector", lambda e: e.reciprocal(out=dcc[:, 5:6], in_=dcc[:, 4:5]),
                          writes=[rdcc])
                    P.run("vector",
                          lambda e: e.tensor_scalar(out=probs[:, 0:S], in0=probs[:, 0:S],
                                                    scalar1=dcc[:, 5:6], scalar2=None,
                                                    op0=ALU.mult),
                          reads=[rdcc], writes=[rprobs])
                    psa, psareg = env.PS[2 + qt % 2], env.rP[2 + qt % 2]
                    psb, psbreg = env.PS[4 + qt % 2], env.rP[4 + qt % 2]
                    for j0 in range(0, NT_B, 4):
                        ti = (j0 // 4) % 2
                        bank, breg = env.TS[ti], env.rT[ti]
                        for j in range(j0, j0 + 4):
                            P.run("tensor",
                                  lambda e, j=j, j0=j0, bank=bank: e.transpose(
                                      bank[:, (j - j0) * 128:(j - j0 + 1) * 128],
                                      probs[:, j * 128:(j + 1) * 128], C.ident_b),
                                  reads=[rprobs, C.reg], writes=[breg])
                        P.run("vector",
                              lambda e, bank=bank: e.tensor_copy(
                                  out=pT[:, 0:512], in_=bank[:, 0:512]),
                              reads=[breg], writes=[rpT])
                        def mmava(e, j0=j0, psa=psa):
                            last = None
                            for j in range(j0, j0 + 4):
                                last = e.matmul(psa[:, 0:128],
                                                vct[:, j * 256:j * 256 + 128],
                                                pT[:, (j - j0) * 128:(j - j0 + 1) * 128],
                                                start=(j == 0), stop=(j == NT_B - 1))
                            return last
                        def mmavb(e, j0=j0, psb=psb):
                            last = None
                            for j in range(j0, j0 + 4):
                                last = e.matmul(psb[:, 0:128],
                                                vct[:, j * 256 + 128:j * 256 + 256],
                                                pT[:, (j - j0) * 128:(j - j0 + 1) * 128],
                                                start=(j == 0), stop=(j == NT_B - 1))
                            return last
                        P.run("tensor", mmava, reads=[rpT, rvc], writes=[psareg])
                        P.run("tensor", mmavb, reads=[rpT, rvc], writes=[psbreg])
                    P.run("scalar",
                          lambda e, h=h, qt=qt, psa=psa: e.copy(
                              out=aout[:, h * 1024 + qt * 128:h * 1024 + (qt + 1) * 128],
                              in_=psa[:, 0:128]),
                          reads=[psareg], writes=[raout])
                    P.run("scalar",
                          lambda e, h=h, qt=qt, psb=psb: e.copy(
                              out=aout[:, h * 1024 + 512 + qt * 128:
                                       h * 1024 + 512 + (qt + 1) * 128],
                              in_=psb[:, 0:128]),
                          reads=[psbreg], writes=[raout])
            import os as _os
            if _os.environ.get("L1F_DBG") and b == 0 and st4 == 0:
                dbg = nc.dram_tensor("dbg", [128, 10240], F32,
                                     kind="ExternalOutput").ap()
                P.run("gpsimd", lambda e: e.dma_start(out=dbg[:, 0:4096], in_=aout[:, 0:4096]), reads=[raout], dma=True)
                P.run("gpsimd", lambda e: e.dma_start(out=dbg[:, 4096:8192], in_=vct[:, 0:4096]), reads=[rvc], dma=True)
                P.run("gpsimd", lambda e: e.dma_start(out=dbg[:, 8192:10240], in_=probs[:, 0:2048]), reads=[rprobs], dma=True)
            for oc in range(8):
                wT_load(P, env, C, ws, W["L1_ffn_Wo"][oc * 128:(oc + 1) * 128, :], 0, 128,
                        wst, rwt, 0)
                psu, preg = env.PS[oc % 2], env.rP[oc % 2]
                def mmo(e, psu=psu):
                    last = None
                    for dc in range(8):
                        last = e.matmul(psu[:], wst[:, dc * 128:(dc + 1) * 128],
                                        aout[:, dc * 512:(dc + 1) * 512],
                                        start=(dc == 0), stop=(dc == 7))
                    return last
                P.run("tensor", mmo, reads=[rwt, raout], writes=[preg])
                cb(b, oc, st4, psu, preg)
    import os
    if os.environ.get("L1F_STOP_AFTER_ATTN"):
        return
    _, rstd2, rrs2 = emit_layernorm(P, env, C, x_sb, xregs, tag="r2", rms=True)
    g2c, rg2c = emit_vec_cols(P, env, C, "g2cols", W["L1_ffn_g2"])
    gates = env.sbuf("l1_gates", [128, 8 * NT], F32)
    rgates = env.reg("l1_gates")
    gcw = env.sbuf("mha_ncols", [128, 16], F32)
    rgcw = env.reg("ncols")
    gidx = env.sbuf("mha_dc", [128, 8], F32)[:].bitcast(U32)[:, 0:8]
    rgidx = env.reg("dcc")

    def logits_cb(i, lps, lpreg):
        P.run("vector",
              lambda e, lps=lps: e.reduce_max(out=gcw[:, 0:1], in_=lps, axis=AX.X,
                                              negate=True),
              reads=[lpreg], writes=[rgcw])
        P.run("scalar",
              lambda e, lps=lps: e.activation(out=gcw[:, 8:16], in_=lps, func=AF.Exp,
                                              bias=gcw[:, 0:1], accum_out=gcw[:, 1:2]),
              reads=[lpreg], writes=[rgcw])
        P.run("vector", lambda e: e.reciprocal(out=gcw[:, 2:3], in_=gcw[:, 1:2]),
              writes=[rgcw])
        P.run("vector",
              lambda e: e.tensor_scalar(out=gcw[:, 8:16], in0=gcw[:, 8:16],
                                        scalar1=gcw[:, 2:3], scalar2=None, op0=ALU.mult),
              writes=[rgcw])
        P.run("vector", lambda e: e.max(out=gcw[:, 0:8], in_=gcw[:, 8:16]), writes=[rgcw])
        P.run("vector",
              lambda e: e.max_index(out=gidx[:, 0:8], in_max=gcw[:, 0:8],
                                    in_values=gcw[:, 8:16]),
              writes=[rgidx])
        P.run("vector",
              lambda e: e.tensor_tensor(out=gcw[:, 2:3], in0=gcw[:, 1:2],
                                        in1=gcw[:, 0:1], op=ALU.subtract),
              writes=[rgcw])
        P.run("scalar",
              lambda e: e.activation(out=gcw[:, 3:4], in_=gcw[:, 2:3], func=AF.Exp),
              writes=[rgcw])
        P.run("vector",
              lambda e: e.tensor_scalar(out=gcw[:, 4:5], in0=gcw[:, 3:4], scalar1=1.0,
                                        scalar2=None, op0=ALU.add),
              writes=[rgcw])
        P.run("vector", lambda e: e.reciprocal(out=gcw[:, 5:6], in_=gcw[:, 4:5]),
              writes=[rgcw])
        P.run("vector",
              lambda e: e.tensor_tensor(out=gcw[:, 6:7], in0=gcw[:, 3:4],
                                        in1=gcw[:, 5:6], op=ALU.mult),
              writes=[rgcw])
        P.run("vector",
              lambda e: e.tensor_copy(out=gcw[:, 8:10], in_=gidx[:, 0:2].bitcast(I32)),
              reads=[rgidx], writes=[rgcw])
        for ee in range(8):
            P.run("vector",
                  lambda e, ee=ee, i=i: e.tensor_single_scalar(
                      out=gates[:, ee * NT + i:ee * NT + i + 1], in_=gcw[:, 8:9],
                      scalar=float(ee), op=ALU.is_equal),
                  writes=[rgates])
            P.run("vector",
                  lambda e, ee=ee, i=i: e.tensor_tensor(
                      out=gates[:, ee * NT + i:ee * NT + i + 1],
                      in0=gates[:, ee * NT + i:ee * NT + i + 1],
                      in1=gcw[:, 5:6], op=ALU.mult),
                  writes=[rgates])
            P.run("vector",
                  lambda e, ee=ee: e.tensor_single_scalar(
                      out=gcw[:, 10:11], in_=gcw[:, 9:10],
                      scalar=float(ee), op=ALU.is_equal),
                  writes=[rgcw])
            P.run("vector",
                  lambda e: e.tensor_tensor(out=gcw[:, 11:12], in0=gcw[:, 10:11],
                                            in1=gcw[:, 6:7], op=ALU.mult),
                  writes=[rgcw])
            P.run("vector",
                  lambda e, ee=ee, i=i: e.tensor_tensor(
                      out=gates[:, ee * NT + i:ee * NT + i + 1],
                      in0=gates[:, ee * NT + i:ee * NT + i + 1],
                      in1=gcw[:, 11:12], op=ALU.add),
                  reads=[rgcw], writes=[rgates])

    gw_sb = vct[:].bitcast(F32)[0:8, 0:1024]
    zhl = kcf[:, 0:2048]
    ebb = env.sbuf("mla_q", [128, 2 * 512], BF16)[:].bitcast(F32)[:, 0:8]
    rebb = env.reg("mla_q")
    ebrow = env.sbuf("mla_q", [128, 2 * 512], BF16)[:].bitcast(F32)[0:1, 8:16]
    rebr = env.reg("mla_q")
    P.run("sync", lambda e: e.dma_start(out=ebrow, in_=W["L1_ffn_expert_bias"][None, :]),
          writes=[rebr], dma=True)
    P.run("tensor",
          lambda e: e.matmul(env.P2[:, 0:8], C.ones_f[0:1, 0:128], ebrow,
                             start=True, stop=True),
          reads=[rebr, C.reg], writes=[env.rP[2]])
    P.run("vector", lambda e: e.tensor_copy(out=ebb, in_=env.P2[:, 0:8]),
          reads=[env.rP[2]], writes=[rebb])

    def finish_cb(i, raw_ps, rawreg):
        P.run("vector",
              lambda e, i=i, raw_ps=raw_ps: e.scalar_tensor_tensor(
                  out=env.P3[:, 8:16], in0=raw_ps, scalar=rstd2[:, i:i + 1],
                  in1=ebb, op0=ALU.mult, op1=ALU.add),
              reads=[rrs2, rebb], writes=[rawreg])
        logits_cb(i, env.P3[:, 8:16], rawreg)

    emit_hilo_router(P, env, C, x_sb, xregs, W["L1_ffn_Wr"],
                     g2c, rg2c, gw_sb, rvc, zhl, rkc, finish_cb)

    import os as _os
    if _os.environ.get("L1F_DBG2"):
        dbg2 = nc.dram_tensor("dbg2", [128, 8 * NT], F32, kind="ExternalOutput").ap()
        P.run("sync", lambda e: e.dma_start(out=dbg2[:], in_=gates[:]),
              reads=[rgates], dma=True)
    h2_dram = env.dram("l1_h2d", [B_LOC, 1024, S], BF16)
    h2t = aout
    rh2t = raout
    sil = probs[:, 0:512]
    rsil = rprobs
    for b in range(B_LOC):
        emit_xnfm(P, env, C, x_sb, xregs, None, rstd2, rrs2, b, xn_fm, rxn)
        for e_i in range(9):
            shared = (e_i == 8)
            if shared:
                w1ap, w3ap, w2ap = W["L1_ffn_sW1"], W["L1_ffn_sW3"], W["L1_ffn_sW2"]
            else:
                w1ap, w3ap, w2ap = (W["L1_ffn_eW1"][e_i], W["L1_ffn_eW3"][e_i],
                                    W["L1_ffn_eW2"][e_i])
            for fc in range(8):
                wT_load(P, env, C, ws, w1ap[fc * 128:(fc + 1) * 128, :], 0, 128, wst, rwt,
                        0, gamma_cols=g2c, rgc=rg2c)
                wT_load(P, env, C, ws, w3ap[fc * 128:(fc + 1) * 128, :], 0, 128, wst, rwt3,
                        8 * 128, gamma_cols=g2c, rgc=rg2c)
                for tch in range(S // 512):
                    def mm1(e, tch=tch):
                        last = None
                        for dc in range(8):
                            last = e.matmul(env.P0[:], wst[:, dc * 128:(dc + 1) * 128],
                                            xn_fm[:, dc * S + tch * 512:
                                                  dc * S + (tch + 1) * 512],
                                            start=(dc == 0), stop=(dc == 7))
                        return last
                    def mm3(e, tch=tch):
                        last = None
                        for dc in range(8):
                            last = e.matmul(env.P1[:], wst[:, 8 * 128 + dc * 128:
                                                          8 * 128 + (dc + 1) * 128],
                                            xn_fm[:, dc * S + tch * 512:
                                                  dc * S + (tch + 1) * 512],
                                            start=(dc == 0), stop=(dc == 7))
                        return last
                    P.run("tensor", mm1, reads=[rwt, rxn], writes=[env.rP[0]])
                    P.run("tensor", mm3, reads=[rwt3, rxn], writes=[env.rP[1]])
                    P.run("scalar",
                          lambda e: e.activation(out=sil, in_=env.P0[:], func=AF.Silu),
                          reads=[env.rP[0]], writes=[rsil])
                    P.run("vector",
                          lambda e, tch=tch: e.tensor_tensor(
                              out=h2t[:, tch * 512:(tch + 1) * 512], in0=sil,
                              in1=env.P1[:], op=ALU.mult),
                          reads=[rsil, env.rP[1]], writes=[rh2t])
                P.run("sync",
                      lambda e, b=b, fc=fc: e.dma_start(
                          out=h2_dram[b, fc * 128:(fc + 1) * 128, :], in_=h2t[:, 0:S]),
                      reads=[rh2t], dma=True)
            for tch in range(S // 512):
                P.run("sync",
                      lambda e, b=b, tch=tch: e.dma_start(
                          out=h2t[:].rearrange("p (c t) -> p c t", t=512),
                          in_=h2_dram[b, :, tch * 512:(tch + 1) * 512]
                              .rearrange("(c p) t -> p c t", p=128)),
                      writes=[rh2t], dma=True)
                for oc in range(8):
                    so = (oc % 2) * 8 * 128
                    rw2 = (rwt, rwt3)[oc % 2]
                    wT_load(P, env, C, ws, w2ap[oc * 128:(oc + 1) * 128, :], 0, 128,
                            wst, rw2, so)
                    psu, preg = env.PS[2 + oc % 2], env.rP[2 + oc % 2]
                    def mm2(e, psu=psu, so=so):
                        last = None
                        for fc in range(8):
                            last = e.matmul(psu[:], wst[:, so + fc * 128:so + (fc + 1) * 128],
                                            h2t[:, fc * 512:(fc + 1) * 512],
                                            start=(fc == 0), stop=(fc == 7))
                        return last
                    P.run("tensor", mm2, reads=[rw2, rh2t], writes=[preg])
                    P.run("vector",
                          lambda e, psu=psu: e.tensor_copy(out=env.scr4k[:, 0:512],
                                                           in_=psu[:]),
                          reads=[preg], writes=[env.rscr])
                    for j in range(4):
                        tb, tbr = (env.P4, env.rP[4]) if j % 2 == 0 else (env.P5, env.rP[5])
                        P.run("tensor",
                              lambda e, j=j, tb=tb: e.transpose(
                                  tb[:, 0:128], env.scr4k[:, j * 128:(j + 1) * 128],
                                  C.ident_f),
                              reads=[env.rscr, C.reg], writes=[tbr])
                        it = b * NT_B + tch * 4 + j
                        if shared:
                            P.run("vector",
                                  lambda e, it=it, oc=oc, tb=tb: e.tensor_tensor(
                                      out=x_sb[:, it * 1024 + oc * 128:
                                               it * 1024 + (oc + 1) * 128],
                                      in0=x_sb[:, it * 1024 + oc * 128:
                                               it * 1024 + (oc + 1) * 128],
                                      in1=tb[:, 0:128], op=ALU.add),
                                  reads=[tbr], writes=[xregs[it]])
                        else:
                            P.run("vector",
                                  lambda e, it=it, oc=oc, tb=tb, e_i=e_i:
                                      e.scalar_tensor_tensor(
                                          out=x_sb[:, it * 1024 + oc * 128:
                                                   it * 1024 + (oc + 1) * 128],
                                          in0=tb[:, 0:128],
                                          scalar=gates[:, e_i * NT + it:e_i * NT + it + 1],
                                          in1=x_sb[:, it * 1024 + oc * 128:
                                                   it * 1024 + (oc + 1) * 128],
                                          op0=ALU.mult, op1=ALU.add),
                                  reads=[tbr, rgates], writes=[xregs[it]])


def emit_l2_attn(P, env, C, x_sb, xregs, W):
    nc = env.nc
    H, HD = 16, 64
    NCH = S // 128
    nm, rstd, rrs = emit_layernorm(P, env, C, x_sb, xregs)
    wqkv, wo = W["L2_attn_wqkv"], W["L2_attn_wo"]
    xn_fm = env.sbuf("xn_fm", [128, 8 * S], BF16)
    rxn = env.reg("xn_fm")
    qk = env.sbuf("mha_probs", [128, S], BF16)
    kk = env.sbuf("mha_kf", [128, S], BF16)
    rqk, rkk = env.reg("probs"), env.reg("kf")
    vt = env.sbuf("mha_vt", [128, 2048], BF16)
    rvt = env.reg("vt")
    yt = env.sbuf("mha_qf", [128, S // 2], BF16)
    ryt = env.reg("qf")
    atm = env.sbuf("mha_pT", [128, 1024], BF16)
    ratm = env.reg("pT")
    stt = env.sbuf("l2a_st", [128, 65], BF16)
    rstt = env.reg("l2a_st")
    attn = env.sbuf("mha_attn", [128, 8 * S // 2], BF16)
    rattn = env.reg("attn")
    wst = env.sbuf("mha_wst", [128, 2 * 8 * 128], BF16)
    rwt = env.reg("wslot0")
    dcc = env.sbuf("mha_dc", [128, 8], F32)
    rdcc = env.reg("dcc")
    ws = WSlice(P, env, "mha", 1024, 1)
    y_dram = env.dram("y_heads", [B_LOC, H, HD, S], BF16)
    rydram = env.reg("ydram")
    cb = make_delta_adder(P, env, C, x_sb, xregs)

    for b in range(B_LOC):
        emit_xnfm(P, env, C, x_sb, xregs, nm, rstd, rrs, b, xn_fm, rxn)
        for h in range(H):
            for (kind, off) in (("q", 0), ("k", 512), ("v", 1024)):
                c0 = {"q": 0, "k": D, "v": 2 * D}[kind] + h * HD
                P.run("gpsimd",
                      lambda e, c0=c0, off=off: e.dma_start(
                          out=wst[:, off:off + 512].rearrange("p (dc o) -> p dc o", o=64),
                          in_=wqkv[:, c0:c0 + 64].rearrange("(dc p) o -> p dc o", p=128)),
                      writes=[rwt], dma=True)
            for (dstb, rdst, off) in ((qk, rqk, 0), (kk, rkk, 512)):
                for tch in range(S // 512):
                    psu, preg = env.PS[tch % 2], env.rP[tch % 2]
                    def mmp(e, off=off, tch=tch, psu=psu):
                        last = None
                        for dc in range(8):
                            last = e.matmul(psu[0:64, :],
                                            wst[:, off + dc * 64:off + (dc + 1) * 64],
                                            xn_fm[:, dc * S + tch * 512:dc * S + (tch + 1) * 512],
                                            start=(dc == 0), stop=(dc == 7))
                        return last
                    P.run("tensor", mmp, reads=[rwt, rxn], writes=[preg])
                    P.run("scalar",
                          lambda e, dstb=dstb, tch=tch, psu=psu: e.activation(
                              out=dstb[0:64, tch * 512:(tch + 1) * 512],
                              in_=psu[0:64, :], func=AF.Relu),
                          reads=[preg], writes=[rdst])
            P.run("vector",
                  lambda e: e.memset(vt[:, 0:NCH * 65].rearrange(
                      "p (c w) -> p c w", w=65)[:, :, 64:65], 1.0),
                  writes=[rvt])
            for it in range(NCH):
                psu, preg = env.PS[2 + it % 2], env.rP[2 + it % 2]
                def mmv(e, it=it, psu=psu):
                    last = None
                    for dc in range(8):
                        last = e.matmul(psu[:, 0:64],
                                        xn_fm[:, dc * S + it * 128:dc * S + (it + 1) * 128],
                                        wst[:, 1024 + dc * 64:1024 + (dc + 1) * 64],
                                        start=(dc == 0), stop=(dc == 7))
                    return last
                P.run("tensor", mmv, reads=[rwt, rxn], writes=[preg])
                P.run("scalar",
                      lambda e, it=it, psu=psu: e.copy(
                          out=vt[:, it * 65:it * 65 + 64], in_=psu[:, 0:64]),
                      reads=[preg], writes=[rvt])
            P.run("vector", lambda e: e.memset(stt[:, 0:65], 0.0), writes=[rstt])
            for c in range(NCH):
                pa, rpa = env.PS[c % 2], env.rP[c % 2]
                py, rpy = env.PS[2 + c % 2], env.rP[2 + c % 2]
                P.run("tensor",
                      lambda e, c=c, pa=pa: e.matmul(
                          pa[:, 0:128], kk[0:64, c * 128:(c + 1) * 128],
                          qk[0:64, c * 128:(c + 1) * 128], start=True, stop=True),
                      reads=[rqk, rkk], writes=[rpa])
                P.run("vector",
                      lambda e, pa=pa: e.tensor_tensor(
                          out=atm[:, 0:128], in0=pa[:, 0:128], in1=C.incl_upper,
                          op=ALU.mult),
                      reads=[rpa, C.reg], writes=[ratm])
                P.run("tensor",
                      lambda e, c=c: e.transpose(env.T0[:, 0:64],
                                                 kk[0:64, c * 128:(c + 1) * 128],
                                                 C.ident_b[0:64, 0:64]),
                      reads=[rkk, C.reg], writes=[env.rT[0]])
                P.run("vector",
                      lambda e: e.tensor_copy(out=atm[:, 128:192], in_=env.T0[:, 0:64]),
                      reads=[env.rT[0]], writes=[ratm])
                def mmy(e, c=c, py=py):
                    last = e.matmul(py[:, 0:65], atm[:, 0:128], vt[:, c * 65:(c + 1) * 65],
                                    start=True, stop=(c == 0))
                    if c > 0:
                        last = e.matmul(py[:, 0:65], qk[0:64, c * 128:(c + 1) * 128],
                                        stt[0:64, 0:65], start=False, stop=True)
                    return last
                P.run("tensor", mmy, reads=[ratm, rvt, rqk, rstt], writes=[rpy])
                P.run("tensor",
                      lambda e, c=c: e.matmul(env.P4[0:64, 0:65], atm[:, 128:192],
                                              vt[:, c * 65:(c + 1) * 65],
                                              start=(c == 0), stop=(c == NCH - 1)),
                      reads=[ratm, rvt], writes=[env.rP[4]])
                P.run("vector",
                      lambda e: e.tensor_copy(out=stt[0:64, 0:65], in_=env.P4[0:64, 0:65]),
                      reads=[env.rP[4]], writes=[rstt])
                P.run("vector",
                      lambda e, py=py: e.tensor_scalar(out=dcc[:, 0:1], in0=py[:, 64:65],
                                                       scalar1=1e-5, scalar2=None,
                                                       op0=ALU.add),
                      reads=[rpy, C.reg], writes=[rdcc])
                P.run("vector", lambda e: e.reciprocal(out=dcc[:, 1:2], in_=dcc[:, 0:1]),
                      writes=[rdcc])
                P.run("vector",
                      lambda e, c=c, py=py: e.tensor_scalar(
                          out=yt[:, c * 64:(c + 1) * 64], in0=py[:, 0:64],
                          scalar1=dcc[:, 1:2], scalar2=None, op0=ALU.mult),
                      reads=[rpy, rdcc], writes=[ryt])
            for it in range(NCH):
                P.run("tensor",
                      lambda e, it=it: e.transpose(env.T1[0:64, 0:128],
                                                   yt[:, it * 64:(it + 1) * 64],
                                                   C.ident_b),
                      reads=[ryt, C.reg], writes=[env.rT[1]])
                P.run("vector",
                      lambda e, it=it: e.tensor_copy(
                          out=attn[0:64, it * 128:(it + 1) * 128], in_=env.T1[0:64, 0:128]),
                      reads=[env.rT[1]], writes=[rattn])
            P.run("gpsimd",
                  lambda e, b=b, h=h: e.dma_start(out=y_dram[b, h], in_=attn[0:64, 0:S]),
                  reads=[rattn], writes=[rydram], dma=True)
        for tch in range(S // 512):
            for h in range(H):
                P.run("gpsimd",
                      lambda e, b=b, h=h, tch=tch: e.dma_start(
                          out=attn[(h % 2) * 64:(h % 2) * 64 + 64,
                                   (h // 2) * 512:(h // 2) * 512 + 512],
                          in_=y_dram[b, h, :, tch * 512:(tch + 1) * 512]),
                      reads=[rydram], writes=[rattn], dma=True)
            for oc in range(8):
                P.run("gpsimd",
                      lambda e, oc=oc: e.dma_start(
                          out=wst[:, 0:1024].rearrange("p (dc o) -> p dc o", o=128),
                          in_=wo[:, oc * 128:(oc + 1) * 128]
                              .rearrange("(dc p) o -> p dc o", p=128)),
                      writes=[rwt], dma=True)
                psu, preg = env.PS[oc % 2], env.rP[oc % 2]
                def mmo(e, psu=psu):
                    last = None
                    for dc in range(8):
                        last = e.matmul(psu[:], wst[:, dc * 128:(dc + 1) * 128],
                                        attn[:, dc * 512:(dc + 1) * 512],
                                        start=(dc == 0), stop=(dc == 7))
                    return last
                P.run("tensor", mmo, reads=[rwt, rattn], writes=[preg])
                cb(b, oc, tch, psu, preg)


BLOCK_WEIGHTS = {
    "l0_attn": ["L0_attn_WQ", "L0_attn_WK", "L0_attn_WV", "L0_attn_WO"],
    "l1_attn": ["L1_attn_Wq", "L1_attn_Wk", "L1_attn_Wv", "L1_attn_Wo"],
    "l2_ffn": ["L2_ffn_n1_w", "L2_ffn_qkv_w", "L2_ffn_o_w", "L2_ffn_n2_w",
               "L2_ffn_gate_w", "L2_ffn_up_w", "L2_ffn_down_w"],
    "l3_ffn": ["L3_ffn_qweights", "L3_ffn_scales", "L3_ffn_biases",
               "L3_ffn_lora_a", "L3_ffn_lora_b", "L3_ffn_ln_gamma", "L3_ffn_ln_beta"],
    "l1_ffn": ["L1_ffn_g1", "L1_ffn_g2", "L1_ffn_Wq", "L1_ffn_Wkc", "L1_ffn_Wvc",
               "L1_ffn_Wo", "L1_ffn_Wr", "L1_ffn_expert_bias", "L1_ffn_sW1",
               "L1_ffn_sW2", "L1_ffn_sW3", "L1_ffn_eW1", "L1_ffn_eW2", "L1_ffn_eW3"],
    "l2_attn": ["L2_attn_wqkv", "L2_attn_wo"],
    "l3_attn": ["L3_attn_w_qkv", "L3_attn_b_qkv", "L3_attn_w_out", "L3_attn_b_out"],
    "l0_ffn": ["L0_ffn_Wr", "L0_ffn_br", "L0_ffn_W1", "L0_ffn_b1", "L0_ffn_W2",
               "L0_ffn_b2"],
}
def emit_l3_attn(P, env, C, x_sb, xregs, W):
    nc = env.nc
    H, HD = 16, 64
    nm, rstd, rrs = emit_layernorm(P, env, C, x_sb, xregs)
    cosps = env.psum("l3a_cos", [128, 1024], F32)
    oeps = env.psum("l3a_oe", [128, 2048], F32)
    spps = env.psum("l3a_sp", [128, 512], F32)
    t0ps = env.psum("l3a_t0", [128, 1024], BF16)
    rcos, roe, rsp, rt0 = (env.reg("rc_cos"), env.reg("rc_oe"),
                           env.reg("rc_sp"), env.reg("rc_t0"))
    env.TS = [t0ps, t0ps]
    env.rT = [rt0, rt0]
    wqkv, bqkv = W["L3_attn_w_qkv"], W["L3_attn_b_qkv"]
    wout, bout = W["L3_attn_w_out"], W["L3_attn_b_out"]
    xn_fm = env.sbuf("xn_fm", [128, 8 * S], BF16)
    rxn = env.reg("xn_fm")
    qn = env.sbuf("l3a_q", [128, S], BF16)
    kn = env.sbuf("mha_kf", [128, S], BF16)
    rqn, rkn = env.reg("l3a_q"), env.reg("kf")
    vt = env.sbuf("mha_vt", [128, S], BF16)
    rvt = env.reg("vt")
    vsc = env.sbuf("l3a_vsc", [128, 65], BF16)
    rvsc = env.reg("l3a_vsc")
    probs = env.sbuf("mha_probs", [128, S], BF16)
    rprobs = env.reg("probs")
    cm = env.sbuf("l3a_cm", [128, 2048], F32)
    rcm = env.reg("l3a_cm")
    wbuf = env.sbuf("l3a_w", [128, 2048], F32)
    rwb = env.reg("l3a_w")
    oesb = cm
    roesb = rcm
    attn = wbuf[:].bitcast(BF16)
    rattn = rwb
    wst = env.sbuf("mha_wst", [128, 2 * 8 * 128], BF16)
    rwt = env.reg("wslot0")
    dcc = env.sbuf("mha_dc", [128, 8], F32)
    rdcc = env.reg("dcc")
    brow = env.sbuf("l3a_br", [1, 192], F32)
    rbr = env.reg("l3a_br")
    ncol = env.sbuf("mha_ncols", [128, 16], F32)
    rnc = env.reg("ncols")
    scrb = env.scr4k[:].bitcast(BF16)
    ws = WSlice(P, env, "mha", 1024, 1)
    y_dram = env.dram("y_heads", [B_LOC, H, HD, S], BF16)
    rydram = env.reg("ydram")

    for b in range(B_LOC):
        emit_xnfm(P, env, C, x_sb, xregs, nm, rstd, rrs, b, xn_fm, rxn)
        for h in range(H):
            P.run("sync",
                  lambda e, h=h: e.dma_start(out=brow[0:1, 0:64],
                                             in_=bqkv[None, h * 64:(h + 1) * 64]),
                  writes=[rbr], dma=True)
            P.run("sync",
                  lambda e, h=h: e.dma_start(out=brow[0:1, 64:128],
                                             in_=bqkv[None, D + h * 64:D + (h + 1) * 64]),
                  writes=[rbr], dma=True)
            P.run("sync",
                  lambda e, h=h: e.dma_start(
                      out=brow[0:1, 128:192],
                      in_=bqkv[None, 2 * D + h * 64:2 * D + (h + 1) * 64]),
                  writes=[rbr], dma=True)
            for (src_i, off) in ((0, 0), (1, 512), (2, 1024)):
                wT_load(P, env, C, ws, wqkv[src_i * D + h * 64:src_i * D + h * 64 + 64, :],
                        0, 64, wst, rwt, off)
            for it in range(NT_B):
                for (off, boff, dst, rdst, normed) in (
                        (0, 0, qn, rqn, True), (512, 64, kn, rkn, True),
                        (1024, 128, vt, rvt, False)):
                    def mmp(e, it=it, off=off, boff=boff):
                        for dc in range(8):
                            e.matmul(spps[:, 0:64],
                                     xn_fm[:, dc * S + it * 128:dc * S + (it + 1) * 128],
                                     wst[:, off + dc * 64:off + (dc + 1) * 64],
                                     start=(dc == 0), stop=False)
                        return e.matmul(spps[:, 0:64], C.ones_f[0:1, 0:128],
                                        brow[0:1, boff:boff + 64],
                                        start=False, stop=True)
                    P.run("tensor", mmp, reads=[rwt, rxn, rbr, C.reg], writes=[rsp])
                    if not normed:
                        P.run("scalar",
                              lambda e, it=it: e.copy(out=vt[:, it * 64:(it + 1) * 64],
                                                      in_=spps[:, 0:64]),
                              reads=[rsp], writes=[rvt])
                        continue
                    P.run("scalar",
                          lambda e: e.activation(out=env.scr4k[:, 0:64], in_=spps[:, 0:64],
                                                 func=AF.Square, accum_out=dcc[:, 0:1]),
                          reads=[rsp], writes=[env.rscr, rdcc])
                    P.run("scalar",
                          lambda e: e.activation(out=dcc[:, 1:2], in_=dcc[:, 0:1],
                                                 func=AF.Sqrt),
                          writes=[rdcc])
                    P.run("vector",
                          lambda e: e.tensor_scalar(out=dcc[:, 2:3], in0=dcc[:, 1:2],
                                                    scalar1=1e-12, scalar2=None,
                                                    op0=ALU.max),
                          writes=[rdcc])
                    P.run("vector", lambda e: e.reciprocal(out=dcc[:, 3:4], in_=dcc[:, 2:3]),
                          writes=[rdcc])
                    P.run("vector",
                          lambda e: e.tensor_scalar(out=scrb[:, 0:64], in0=spps[:, 0:64],
                                                    scalar1=dcc[:, 3:4], scalar2=None,
                                                    op0=ALU.mult),
                          reads=[rsp, rdcc], writes=[env.rscr])
                    P.run("tensor",
                          lambda e: e.transpose(t0ps[0:64, 0:128], scrb[:, 0:64],
                                                C.ident_b),
                          reads=[env.rscr, C.reg], writes=[rt0])
                    P.run("vector",
                          lambda e, it=it, dst=dst: e.tensor_copy(
                              out=dst[0:64, it * 128:(it + 1) * 128],
                              in_=t0ps[0:64, 0:128]),
                          reads=[rt0], writes=[rdst])
            for tt in range(NT_B):
                for half in range(2):
                    def mmc(e, tt=tt, half=half):
                        e.matmul(cosps[:, 0:512],
                                 kn[0:64, tt * 128:(tt + 1) * 128],
                                 qn[0:64, half * 1024:half * 1024 + 512],
                                 start=True, stop=True)
                        return e.matmul(cosps[:, 512:1024],
                                        kn[0:64, tt * 128:(tt + 1) * 128],
                                        qn[0:64, half * 1024 + 512:(half + 1) * 1024],
                                        start=True, stop=True)
                    P.run("tensor", mmc, reads=[rqn, rkn], writes=[rcos])
                    P.run("vector",
                          lambda e, half=half: e.tensor_scalar(
                              out=cm[:, half * 1024:(half + 1) * 1024],
                              in0=cosps[:, 0:1024],
                              scalar1=float(-1.0 + 1e-7), scalar2=float(1.0 - 1e-7),
                              op0=ALU.max, op1=ALU.min),
                          reads=[rcos], writes=[rcm])
                P.run("scalar",
                      lambda e: e.activation(out=wbuf[:], in_=cm[:], func=AF.Square),
                      reads=[rcm], writes=[rwb])
                P.run("scalar",
                      lambda e: e.activation(out=wbuf[:], in_=wbuf[:], func=AF.Ln,
                                             scale=-1.0, bias=C.col(1.0)),
                      reads=[C.reg], writes=[rwb])
                P.run("scalar",
                      lambda e: e.activation(out=wbuf[:], in_=wbuf[:], func=AF.Exp,
                                             scale=-0.5),
                      writes=[rwb])
                P.run("vector",
                      lambda e: e.tensor_tensor(out=wbuf[:], in0=cm[:], in1=wbuf[:],
                                                op=ALU.mult),
                      reads=[rcm], writes=[rwb])
                P.run("scalar",
                      lambda e: e.activation(out=wbuf[:], in_=wbuf[:], func=AF.Arctan),
                      writes=[rwb])
                P.run("scalar",
                      lambda e: e.activation(out=wbuf[:], in_=wbuf[:], func=AF.Ln,
                                             scale=-1.0, bias=C.col(float(1.0 + np.pi / 2))),
                      reads=[C.reg], writes=[rwb])
                P.run("scalar",
                      lambda e: e.activation(out=probs[:], in_=wbuf[:], func=AF.Exp,
                                             scale=-3.2, accum_out=ncol[:, 0:1]),
                      reads=[C.reg], writes=[rprobs, rnc])
                P.run("vector", lambda e: e.reciprocal(out=ncol[:, 1:2], in_=ncol[:, 0:1]),
                      reads=[rnc], writes=[rnc])
                P.run("vector",
                      lambda e, tt=tt: e.tensor_scalar(
                          out=vsc[:, 0:64], in0=vt[:, tt * 64:(tt + 1) * 64],
                          scalar1=ncol[:, 1:2], scalar2=None, op0=ALU.mult),
                      reads=[rvt, rnc], writes=[rvsc])
                P.run("vector",
                      lambda e: e.tensor_copy(out=vsc[:, 64:65], in_=ncol[:, 1:2]),
                      reads=[rnc], writes=[rvsc])
                def mmav(e, tt=tt):
                    last = None
                    for sc in range(4):
                        last = e.matmul(oeps[0:65, sc * 512:(sc + 1) * 512],
                                        vsc[:, 0:65],
                                        probs[:, sc * 512:(sc + 1) * 512],
                                        start=(tt == 0), stop=(tt == NT_B - 1))
                    return last
                P.run("tensor", mmav, reads=[rvsc, rprobs], writes=[roe])
            P.run("vector",
                  lambda e: e.tensor_copy(out=oesb[0:65, :], in_=oeps[0:65, :]),
                  reads=[roe], writes=[roesb])
            for it in range(NT_B):
                P.run("tensor",
                      lambda e, it=it: e.transpose(
                          cosps[:, 0:65], oesb[0:65, it * 128:(it + 1) * 128],
                          C.ident_f[0:65, 0:65]),
                      reads=[roesb, C.reg], writes=[rcos])
                P.run("vector",
                      lambda e: e.reciprocal(out=dcc[:, 4:5], in_=cosps[:, 64:65]),
                      reads=[rcos], writes=[rdcc])
                P.run("vector",
                      lambda e, it=it: e.tensor_scalar(
                          out=scrb[:, 0:64], in0=cosps[:, 0:64],
                          scalar1=dcc[:, 4:5], scalar2=None, op0=ALU.mult),
                      reads=[rcos, rdcc], writes=[env.rscr])
                P.run("tensor",
                      lambda e: e.transpose(t0ps[64:128, 0:128], scrb[:, 0:64], C.ident_b,
                                            tile_position=(0, 64)),
                      reads=[env.rscr, C.reg], writes=[rt0])
                P.run("vector",
                      lambda e, it=it: e.tensor_copy(
                          out=qn[64:128, it * 128:(it + 1) * 128], in_=t0ps[64:128, 0:128]),
                      reads=[rt0], writes=[rqn])
            P.run("gpsimd",
                  lambda e, b=b, h=h: e.dma_start(out=y_dram[b, h], in_=qn[64:128, 0:S]),
                  reads=[rqn], writes=[rydram], dma=True)
            import os as _os
            if _os.environ.get("L3A_DBG") and b == 0 and h == 0:
                dbg = nc.dram_tensor("dbg", [128, 2048 + 2048 + 16 + 2048], F32,
                                     kind="ExternalOutput").ap()
                P.run("gpsimd", lambda e: e.dma_start(out=dbg[:, 0:2048], in_=probs[:]),
                      reads=[rprobs], dma=True)
                P.run("gpsimd", lambda e: e.dma_start(out=dbg[:, 2048:4096], in_=qn[64:128, 0:S].bitcast(F32)[:, 0:1024] if False else qn[:, 0:2048]),
                      reads=[rqn], dma=True)
                P.run("gpsimd", lambda e: e.dma_start(out=dbg[:, 4096:4112], in_=ncol[:]),
                      reads=[rnc], dma=True)
                P.run("gpsimd", lambda e: e.dma_start(out=dbg[:, 4112:6160], in_=kn[:, 0:2048]),
                      reads=[rkn], dma=True)
        for tch in range(S // 512):
            for h in range(H):
                P.run("gpsimd",
                      lambda e, b=b, h=h, tch=tch: e.dma_start(
                          out=attn[(h % 2) * 64:(h % 2) * 64 + 64,
                                   (h // 2) * 512:(h // 2) * 512 + 512],
                          in_=y_dram[b, h, :, tch * 512:(tch + 1) * 512]),
                      reads=[rydram], writes=[rattn], dma=True)
            for oc in range(8):
                wT_load(P, env, C, ws, wout[oc * 128:(oc + 1) * 128, :], 0, 128, wst, rwt, 0)
                P.run("sync",
                      lambda e, oc=oc: e.dma_start(
                          out=ncol[:, 8:9],
                          in_=bout[oc * 128:(oc + 1) * 128][:, None],
                          allow_slow_non_contiguous=True),
                      writes=[rnc], dma=True)
                def mmo(e, oc=oc, tch=tch):
                    last = None
                    for dc in range(8):
                        last = e.matmul(oeps[:, 0:512], wst[:, dc * 128:(dc + 1) * 128],
                                        attn[:, dc * 512:(dc + 1) * 512],
                                        start=(dc == 0), stop=(dc == 7))
                    return last
                P.run("tensor", mmo, reads=[rwt, rattn], writes=[roe])
                P.run("vector",
                      lambda e: e.tensor_scalar(out=env.scr4k[:, 0:512],
                                                in0=oeps[:, 0:512],
                                                scalar1=ncol[:, 8:9], scalar2=None,
                                                op0=ALU.add),
                      reads=[roe, rnc], writes=[env.rscr])
                for j in range(4):
                    it = b * NT_B + (tch * 512 + j * 128) // 128
                    P.run("tensor",
                          lambda e, j=j: e.transpose(
                              cosps[:, 0:128], env.scr4k[:, j * 128:(j + 1) * 128],
                              C.ident_f),
                          reads=[env.rscr, C.reg], writes=[rcos])
                    P.run("vector",
                          lambda e, it=it, oc=oc: e.tensor_tensor(
                              out=x_sb[:, it * 1024 + oc * 128:it * 1024 + (oc + 1) * 128],
                              in0=x_sb[:, it * 1024 + oc * 128:it * 1024 + (oc + 1) * 128],
                              in1=cosps[:, 0:128], op=ALU.add),
                          reads=[rcos], writes=[xregs[it]])



def emit_l0_ffn(P, env, C, x_sb, xregs, W):
    nc = env.nc
    E, CAP = 8, 320
    NSLOT = E * B_LOC * CAP
    nm, rstd, rrs = emit_layernorm(P, env, C, x_sb, xregs)
    wr_ap, br_ap = W["L0_ffn_Wr"], W["L0_ffn_br"]
    w1_ap, b1_ap = W["L0_ffn_W1"], W["L0_ffn_b1"]
    w2_ap, b2_ap = W["L0_ffn_W2"], W["L0_ffn_b2"]

    h_sb = env.sbuf("l0f_h", [128, 20480], BF16)
    rh = env.reg("l0f_h")
    hF = h_sb[:].bitcast(F32)
    xbfm = env.sbuf("l0f_xbfm", [128, 8 * 640], BF16)
    rxbfm = env.reg("l0f_xbfm")
    wstg = env.sbuf("l0f_wst", [128, 2 * 1024], BF16)
    rwst = env.reg("l0f_wst")
    g1b = env.sbuf("l0f_g1", [128, 1024], F32)
    rg1b = env.reg("l0f_g1")
    rows = env.sbuf("l0f_rows", [1, 1152], F32)
    rrows = env.reg("l0f_rows")
    gates = env.sbuf("l0f_gates", [128, 2 * NT], F32)
    rgates = env.reg("l0f_gates")
    gidx = env.sbuf("l0f_gidx", [128, 2 * NT], I32)
    rgidx = env.reg("l0f_gidx")
    sidx = env.sbuf("l0f_sidx", [128, 2], I32)
    rsidx = env.reg("l0f_sidx")
    gcw = env.sbuf("l0f_gcw", [128, 32], F32)
    rgcw = env.reg("l0f_gcw")
    gidx8 = env.sbuf("l0f_gidx8", [128, 8], U32)
    rgidx8 = env.reg("l0f_gidx8")
    runrow = env.sbuf("l0f_run", [1, 8], F32)
    rrun = env.reg("l0f_run")
    mrc = env.sbuf("l0f_mrc", [128, NT], F32)
    rmrc = env.reg("l0f_mrc")
    cswb = env.sbuf("l0f_csw", [128, 16], F32)
    rcsw = env.reg("l0f_csw")
    scrb = env.scr4k[:].bitcast(BF16)
    xnt = env.sbuf("l0f_xnt", [128, 1024], BF16)
    rxnt = env.reg("l0f_xnt")

    xbuf = env.dram("l0f_xbuf", [NSLOT + 1, D], BF16)
    ybuf = env.dram("l0f_ybuf", [NSLOT + 1, D], F32)
    bc_reg = [None]
    def get_bc(e):
        if bc_reg[0] is None:
            bc_reg[0] = e.to_reg(NSLOT - 1)
        return bc_reg[0]
    rxd = env.reg("l0f_xbufd")
    ryd = env.reg("l0f_ybufd")

    P.run("vector",
          lambda e: e.tensor_tensor(out=mrc[:], in0=env.sb["ln_nm"][:],
                                    in1=env.sb["ln_rs"][:], op=ALU.mult),
          reads=[rrs], writes=[rmrc])
    wrTl = env.sbuf("rt_wrT", [128, 8 * 8], F32)
    rwrTl = env.reg("rt_wrT")
    P.run("sync",
          lambda e: e.dma_start(out=wrTl[:].rearrange("p (c e) -> p c e", e=8),
                                in_=wr_ap.rearrange("(c p) e -> p c e", p=128)),
          writes=[rwrTl], dma=True)
    def mmcsw(e):
        last = None
        for c in range(8):
            last = e.matmul(env.P4[0:8, 0:1], wrTl[:, c * 8:(c + 1) * 8],
                            C.ones_f[:, 0:1], start=(c == 0), stop=(c == 7))
        return last
    P.run("tensor", mmcsw, reads=[rwrTl, C.reg], writes=[env.rP[4]])
    P.run("vector", lambda e: e.tensor_copy(out=cswb[0:8, 8:9], in_=env.P4[0:8, 0:1]),
          reads=[env.rP[4]], writes=[rcsw])
    P.run("tensor",
          lambda e: e.transpose(env.P4[0:1, 0:8], cswb[0:8, 8:9], C.ident_f[0:8, 0:8]),
          reads=[rcsw, C.reg], writes=[env.rP[4]])
    P.run("vector",
          lambda e: e.tensor_copy(out=runrow[0:1, 0:8], in_=env.P4[0:1, 0:8]),
          reads=[env.rP[4]], writes=[rrun])
    P.run("tensor",
          lambda e: e.matmul(env.P4[:, 0:8], C.ones_f[0:1, 0:128], runrow[0:1, 0:8],
                             start=True, stop=True),
          reads=[rrun, C.reg], writes=[env.rP[4]])
    P.run("vector", lambda e: e.tensor_copy(out=cswb[:, 0:8], in_=env.P4[:, 0:8]),
          reads=[env.rP[4]], writes=[rcsw])
    zhl = h_sb[:, 4096:6144]
    gw = hF[0:8, 0:1024]
    P.run("sync", lambda e: e.dma_start(out=runrow[0:1, 0:8], in_=br_ap[None, :]),
          writes=[rrun], dma=True)
    P.run("tensor",
          lambda e: e.matmul(env.P4[:, 0:8], C.ones_f[0:1, 0:128], runrow[0:1, 0:8],
                             start=True, stop=True),
          reads=[rrun, C.reg], writes=[env.rP[4]])
    P.run("vector", lambda e: e.tensor_copy(out=cswb[:, 8:16], in_=env.P4[:, 0:8]),
          reads=[env.rP[4]], writes=[rcsw])

    def finish_cb(i, raw_ps, rawreg):
        b = i // NT_B
        il = i % NT_B
        P.run("vector",
              lambda e, i=i, raw_ps=raw_ps: e.tensor_scalar(
                  out=env.P3[:, 8:16], in0=raw_ps, scalar1=rstd[:, i:i + 1],
                  scalar2=None, op0=ALU.mult),
              reads=[rrs], writes=[rawreg])
        P.run("vector",
              lambda e, i=i: e.scalar_tensor_tensor(
                  out=env.P3[:, 8:16], in0=cswb[:, 0:8], scalar=mrc[:, i:i + 1],
                  in1=env.P3[:, 8:16], op0=ALU.mult, op1=ALU.add),
              reads=[rcsw, rmrc], writes=[rawreg])
        P.run("vector",
              lambda e: e.tensor_tensor(out=gcw[:, 0:8], in0=env.P3[:, 8:16],
                                        in1=cswb[:, 8:16], op=ALU.add),
              reads=[rawreg, rcsw], writes=[rgcw])
        P.run("vector",
              lambda e: e.reduce_max(out=gcw[:, 16:17], in_=gcw[:, 0:8], axis=AX.X,
                                     negate=True),
              writes=[rgcw])
        P.run("scalar",
              lambda e: e.activation(out=gcw[:, 8:16], in_=gcw[:, 0:8], func=AF.Exp,
                                     bias=gcw[:, 16:17], accum_out=gcw[:, 17:18]),
              writes=[rgcw])
        P.run("vector", lambda e: e.reciprocal(out=gcw[:, 18:19], in_=gcw[:, 17:18]),
              writes=[rgcw])
        P.run("vector",
              lambda e: e.tensor_scalar(out=gcw[:, 8:16], in0=gcw[:, 8:16],
                                        scalar1=gcw[:, 18:19], scalar2=None,
                                        op0=ALU.mult),
              writes=[rgcw])
        P.run("vector", lambda e: e.max(out=gcw[:, 16:24], in_=gcw[:, 8:16]),
              writes=[rgcw])
        P.run("vector",
              lambda e: e.max_index(out=gidx8[:], in_max=gcw[:, 16:24],
                                    in_values=gcw[:, 8:16]),
              reads=[rgcw], writes=[rgidx8])
        P.run("vector",
              lambda e: e.tensor_copy(out=gcw[:, 24:26], in_=gidx8[:, 0:2].bitcast(I32)),
              reads=[rgidx8], writes=[rgcw])
        P.run("vector",
              lambda e: e.tensor_scalar(out=gcw[:, 26:27].broadcast_to([128, 1]) if False
                                        else env.scr4k[:, 0:8], in0=C.iota8,
                                        scalar1=gcw[:, 24:25], scalar2=None,
                                        op0=ALU.is_equal),
              reads=[C.reg, rgcw], writes=[env.rscr])
        P.run("vector",
              lambda e: e.tensor_scalar(out=env.scr4k[:, 8:16], in0=C.iota8,
                                        scalar1=gcw[:, 25:26], scalar2=None,
                                        op0=ALU.is_equal),
              reads=[C.reg, rgcw], writes=[env.rscr])
        P.run("vector",
              lambda e: e.tensor_tensor(out=env.scr4k[:, 16:24], in0=env.scr4k[:, 0:8],
                                        in1=env.scr4k[:, 8:16], op=ALU.add),
              writes=[env.rscr])
        def mmpos(e, il=il):
            first = True
            if il > 0:
                e.matmul(env.P5[:, 0:8], C.ones_f[0:1, 0:128], runrow[0:1, 0:8],
                         start=True, stop=False)
                first = False
            return e.matmul(env.P5[:, 0:8], C.strict_upper, env.scr4k[:, 16:24],
                            start=first, stop=True)
        P.run("tensor", mmpos, reads=[env.rscr, rrun, C.reg], writes=[env.rP[5]])
        P.run("tensor",
              lambda e: e.matmul(env.P5[0:1, 8:16], C.ones_f[:, 0:1],
                                 env.scr4k[:, 16:24], start=True, stop=True),
              reads=[env.rscr, C.reg], writes=[env.rP[5]])
        if il == 0:
            P.run("vector",
                  lambda e: e.tensor_copy(out=runrow[0:1, 0:8], in_=env.P5[0:1, 8:16]),
                  reads=[env.rP[5]], writes=[rrun])
        else:
            P.run("vector",
                  lambda e: e.tensor_tensor(out=runrow[0:1, 0:8], in0=runrow[0:1, 0:8],
                                            in1=env.P5[0:1, 8:16], op=ALU.add),
                  reads=[env.rP[5]], writes=[rrun])
        for k in range(2):
            oh = env.scr4k[:, k * 8:(k + 1) * 8]
            P.run("vector",
                  lambda e, oh=oh: e.tensor_tensor(out=env.scr4k[:, 24:32], in0=oh,
                                                   in1=env.P5[:, 0:8], op=ALU.mult),
                  reads=[env.rP[5], env.rscr], writes=[env.rscr])
            P.run("vector",
                  lambda e, k=k: e.reduce_sum(out=gcw[:, 26 + k:27 + k],
                                              in_=env.scr4k[:, 24:32], axis=AX.X),
                  writes=[rgcw])
        for k in range(2):
            posc = gcw[:, 26 + k:27 + k]
            P.run("vector",
                  lambda e, posc=posc, k=k: e.tensor_scalar(
                      out=gcw[:, 28 + k:29 + k], in0=posc, scalar1=319.5,
                      scalar2=None, op0=ALU.is_lt),
                  writes=[rgcw])
            P.run("vector",
                  lambda e, k=k, i=i: e.tensor_tensor(
                      out=gates[:, k * NT + i:k * NT + i + 1],
                      in0=gcw[:, 16 + k:17 + k], in1=gcw[:, 28 + k:29 + k],
                      op=ALU.mult),
                  reads=[rgcw], writes=[rgates])
            P.run("vector",
                  lambda e, posc=posc: e.tensor_scalar(out=gcw[:, 30:31], in0=posc,
                                                       scalar1=319.0, scalar2=None,
                                                       op0=ALU.min),
                  writes=[rgcw])
            P.run("vector",
                  lambda e, k=k: e.scalar_tensor_tensor(
                      out=gcw[:, 31:32], in0=gcw[:, 24 + k:25 + k], scalar=640.0,
                      in1=gcw[:, 30:31], op0=ALU.mult, op1=ALU.add),
                  writes=[rgcw])
            if b == 1:
                P.run("vector",
                      lambda e: e.tensor_scalar(out=gcw[:, 31:32], in0=gcw[:, 31:32],
                                                scalar1=320.0, scalar2=None, op0=ALU.add),
                      writes=[rgcw])
            P.run("vector",
                  lambda e, k=k: e.scalar_tensor_tensor(
                      out=gcw[:, 30:31], in0=gcw[:, 28 + k:29 + k], scalar=-1e6,
                      in1=gcw[:, 31:32], op0=ALU.mult, op1=ALU.add),
                  writes=[rgcw])
            P.run("vector",
                  lambda e: e.tensor_scalar(out=gcw[:, 30:31], in0=gcw[:, 30:31],
                                            scalar1=1e6, scalar2=None, op0=ALU.add),
                  writes=[rgcw])
            P.run("vector",
                  lambda e, k=k: e.tensor_copy(out=sidx[:, k:k + 1], in_=gcw[:, 30:31]),
                  reads=[rgcw], writes=[rsidx])
            P.run("vector",
                  lambda e, k=k: e.tensor_tensor(out=gcw[:, 30:31], in0=gcw[:, 31:32],
                                                 in1=gcw[:, 28 + k:29 + k], op=ALU.mult),
                  writes=[rgcw])
            P.run("vector",
                  lambda e, k=k: e.tensor_scalar(out=gcw[:, 29:30],
                                                 in0=gcw[:, 28 + k:29 + k],
                                                 scalar1=-5120.0, scalar2=None,
                                                 op0=ALU.mult),
                  writes=[rgcw])
            P.run("vector",
                  lambda e: e.tensor_tensor(out=gcw[:, 30:31], in0=gcw[:, 30:31],
                                            in1=gcw[:, 29:30], op=ALU.add),
                  writes=[rgcw])
            P.run("vector",
                  lambda e: e.tensor_scalar(out=gcw[:, 30:31], in0=gcw[:, 30:31],
                                            scalar1=5120.0, scalar2=None, op0=ALU.add),
                  writes=[rgcw])
            P.run("vector",
                  lambda e, k=k, i=i: e.tensor_copy(
                      out=gidx[:, k * NT + i:k * NT + i + 1], in_=gcw[:, 30:31]),
                  reads=[rgcw], writes=[rgidx])
        emit_norm_tile(P, env, C, x_sb, xregs, nm, rstd, rrs, i, xnt[:], rxnt)
        for k in range(2):
            def scat_fn(e, k=k, i=i):
                return e.indirect_dma_start(
                    out=xbuf[:], out_offset=IndirectOffsetOnAxis(sidx[:, k:k + 1], 0),
                    in_=xnt[:], in_offset=None,
                    bounds_check=get_bc(e), oob_is_err=False)
            P.run("gpsimd", scat_fn, reads=[rxnt, rsidx], writes=[rxd], dma=True)

    emit_hilo_router(P, env, C, x_sb, xregs, wr_ap, None, None, gw, rh,
                     zhl, rh, finish_cb, load_gw=False, wr_dmajor=True)

    import os as _os
    if _os.environ.get("L0F_STOP") == "router":
        return
    P.run("vector", lambda e: e.memset(env.scr4k[:, 0:1024], 0.0), writes=[env.rscr])
    P.run("sync", lambda e: e.dma_start(out=ybuf[NSLOT:NSLOT + 1, :],
                                        in_=env.scr4k[0:1, 0:1024]),
          reads=[env.rscr], writes=[ryd], dma=True)

    for e_i in range(E):
        P.run("gpsimd",
              lambda e, e_i=e_i: e.dma_start(
                  out=h_sb[:, 0:5120].rearrange("p (t d) -> p t d", d=1024),
                  in_=xbuf[e_i * 640:(e_i + 1) * 640, :]
                      .rearrange("(t p) d -> p t d", p=128)),
              reads=[rxd], writes=[rh], dma=True)
        for st in range(5):
            for dc in range(8):
                transpose128(P, env, C,
                             xbfm[:, dc * 640 + st * 128:dc * 640 + (st + 1) * 128],
                             h_sb[:, st * 1024 + dc * 128:st * 1024 + (dc + 1) * 128],
                             [rh], rxbfm, ti=dc % 2)
        P.run("vector", lambda e: e.memset(rows[0:1, 0:640], 1.0), writes=[rrows])
        for hc in range(32):
            P.run("gpsimd",
                  lambda e, e_i=e_i, hc=hc: e.dma_start(
                      out=wstg[:, 0:1024].rearrange("p (dc o) -> p dc o", o=128),
                      in_=w1_ap[e_i, :, hc * 128:(hc + 1) * 128]
                          .rearrange("(dc p) o -> p dc o", p=128)),
                  writes=[rwst], dma=True)
            P.run("sync",
                  lambda e, e_i=e_i, hc=hc: e.dma_start(
                      out=rows[0:1, 640:768],
                      in_=b1_ap[None, e_i, hc * 128:(hc + 1) * 128]),
                  writes=[rrows], dma=True)
            pi = hc % 2
            psu, preg = env.PS[pi], env.rP[pi]
            def mmh(e, hc=hc, psu=psu):
                for dc in range(8):
                    e.matmul(psu[:, 0:512], wstg[:, dc * 128:(dc + 1) * 128],
                             xbfm[:, dc * 640:dc * 640 + 512],
                             start=(dc == 0), stop=False)
                return e.matmul(psu[:, 0:512],
                                rows[0:1, 640:768], rows[0:1, 0:512],
                                start=False, stop=True)
            psu2, preg2 = env.PS[2 + pi], env.rP[2 + pi]
            def mmh2b(e, hc=hc, psu2=psu2):
                for dc in range(8):
                    e.matmul(psu2[:, 0:128], wstg[:, dc * 128:(dc + 1) * 128],
                             xbfm[:, dc * 640 + 512:dc * 640 + 640],
                             start=(dc == 0), stop=False)
                return e.matmul(psu2[:, 0:128],
                                rows[0:1, 640:768], rows[0:1, 0:128],
                                start=False, stop=True)
            P.run("tensor", mmh, reads=[rwst, rxbfm, rrows, C.reg], writes=[preg])
            P.run("tensor", mmh2b, reads=[rwst, rxbfm, rrows, C.reg], writes=[preg2])
            P.run("scalar",
                  lambda e, hc=hc, psu=psu: e.activation(
                      out=h_sb[:, hc * 640:hc * 640 + 512], in_=psu[:, 0:512],
                      func=AF.Relu),
                  reads=[preg], writes=[rh])
            P.run("scalar",
                  lambda e, hc=hc, psu2=psu2: e.activation(
                      out=h_sb[:, hc * 640 + 512:hc * 640 + 640], in_=psu2[:, 0:128],
                      func=AF.Relu),
                  reads=[preg2], writes=[rh])
        for dh in range(2):
            P.run("sync",
                  lambda e, e_i=e_i, dh=dh: e.dma_start(
                      out=rows[0:1, 640:1152],
                      in_=b2_ap[None, e_i, dh * 512:(dh + 1) * 512]),
                  writes=[rrows], dma=True)
            for hc in range(32):
                P.run("gpsimd",
                      lambda e, e_i=e_i, hc=hc, dh=dh: e.dma_start(
                          out=wstg[:, 1024:1536],
                          in_=w2_ap[e_i, hc * 128:(hc + 1) * 128,
                                    dh * 512:(dh + 1) * 512]),
                      writes=[rwst], dma=True)
                def mmy(e, hc=hc, dh=dh):
                    last = None
                    for st in range(5):
                        last = e.matmul(env.PS[st][:, 0:512],
                                        h_sb[:, hc * 640 + st * 128:hc * 640 + (st + 1) * 128],
                                        wstg[:, 1024:1536],
                                        start=(hc == 0), stop=False)
                    if hc == 31:
                        for st in range(5):
                            last = e.matmul(env.PS[st][:, 0:512],
                                            C.ones_f[0:1, 0:128], rows[0:1, 640:1152],
                                            start=False, stop=True)
                    return last
                P.run("tensor", mmy,
                      reads=[rwst, rh, rrows, C.reg],
                      writes=[env.rP[0], env.rP[1], env.rP[2], env.rP[3], env.rP[4]])
            for st in range(5):
                P.run("vector",
                      lambda e, st=st: e.tensor_copy(out=env.scr4k[:, 0:512],
                                                     in_=env.PS[st][:, 0:512]),
                      reads=[env.rP[st]], writes=[env.rscr])
                P.run("sync",
                      lambda e, e_i=e_i, st=st, dh=dh: e.dma_start(
                          out=ybuf[e_i * 640 + st * 128:e_i * 640 + (st + 1) * 128,
                                   dh * 512:(dh + 1) * 512],
                          in_=env.scr4k[:, 0:512]),
                      reads=[env.rscr], writes=[ryd], dma=True)

    if _os.environ.get("L0F_STOP") == "experts":
        return
    g1sb = env.sbuf("l0f_g1b", [128, 1024], F32)
    rg1sb = env.reg("l0f_g1b")
    if _os.environ.get("L0F_DBG"):
        dbg = nc.dram_tensor("dbg", [128, 4 * NT], F32, kind="ExternalOutput").ap()
        P.run("gpsimd", lambda e: e.dma_start(out=dbg[:, 0:2 * NT], in_=gates[:]),
              reads=[rgates], dma=True)
        P.run("gpsimd", lambda e: e.dma_start(out=dbg[:, 2 * NT:4 * NT], in_=gidx[:]),
              reads=[rgidx], dma=True)
        dbg2 = nc.dram_tensor("dbg2", [256, D], F32, kind="ExternalOutput").ap()
        dbstg = env.scr4k
        rdbs = env.rscr
        P.run("gpsimd", lambda e: e.dma_start(out=dbstg[:], in_=xbuf[0:128, :]),
              reads=[rxd], writes=[rdbs], dma=True)
        P.run("gpsimd", lambda e: e.dma_start(out=dbg2[0:128, :], in_=dbstg[:]),
              reads=[rdbs], dma=True)
        P.run("gpsimd", lambda e: e.dma_start(out=dbstg[:], in_=ybuf[0:128, :]),
              reads=[ryd], writes=[rdbs], dma=True)
        P.run("gpsimd", lambda e: e.dma_start(out=dbg2[128:256, :], in_=dbstg[:]),
              reads=[rdbs], dma=True)
    NGATHER = int(_os.environ.get("L0F_NG", NT))
    for i in range(NGATHER):
        P.run("gpsimd",
              lambda e, i=i: e.indirect_dma_start(
                  out=g1b[:], out_offset=None, in_=ybuf[:],
                  in_offset=IndirectOffsetOnAxis(gidx[:, i:i + 1], 0)),
              reads=[ryd, rgidx], writes=[rg1b], dma=True)
        if _os.environ.get("L0F_NO2") is None:
            P.run("gpsimd",
                  lambda e, i=i: e.indirect_dma_start(
                      out=g1sb[:], out_offset=None, in_=ybuf[:],
                      in_offset=IndirectOffsetOnAxis(gidx[:, NT + i:NT + i + 1], 0)),
                  reads=[ryd, rgidx], writes=[rg1sb], dma=True)
        P.run("vector",
              lambda e, i=i: e.scalar_tensor_tensor(
                  out=x_sb[:, i * 1024:(i + 1) * 1024], in0=g1b[:],
                  scalar=gates[:, i:i + 1],
                  in1=x_sb[:, i * 1024:(i + 1) * 1024], op0=ALU.mult, op1=ALU.add),
              reads=[rg1b, rgates], writes=[xregs[i]])
        P.run("vector",
              lambda e, i=i: e.scalar_tensor_tensor(
                  out=x_sb[:, i * 1024:(i + 1) * 1024], in0=g1sb[:],
                  scalar=gates[:, NT + i:NT + i + 1],
                  in1=x_sb[:, i * 1024:(i + 1) * 1024], op0=ALU.mult, op1=ALU.add),
              reads=[rg1sb, rgates], writes=[xregs[i]])


BLOCK_PSUM = {"l3_attn": "custom"}
BLOCK_EMITTERS = {"l0_attn": emit_l0_attn, "l1_attn": emit_l1_attn,
                  "l2_ffn": emit_l2_ffn, "l3_ffn": emit_l3_ffn,
                  "l1_ffn": emit_l1_ffn, "l2_attn": emit_l2_attn,
                  "l3_attn": emit_l3_attn, "l0_ffn": emit_l0_ffn}


def build_block_program(block_names, weight_specs):
    nc = bass.Bass("TRN2", target_bir_lowering=False, debug=False)
    x_in = nc.dram_tensor("x", [NTOK, D], F32, kind="ExternalInput").ap()
    x_out = nc.dram_tensor("x_out", [NTOK, D], F32, kind="ExternalOutput").ap()
    with contextlib.ExitStack() as stack:
        C = Consts(nc, stack)
        W = {}
        needed = set()
        for bn in block_names:
            needed.update(BLOCK_WEIGHTS[bn])
        for name in sorted(needed):
            shape, dtype = weight_specs[name]
            W[name] = nc.dram_tensor(name, list(shape),
                                     {"float32": F32, "int32": I32}[dtype],
                                     kind="ExternalInput").ap()
        block = stack.enter_context(nc.Block())
        P = Prog(nc, block, stack)
        env = Env(nc, block, stack)
        setup_psum(env)
        x_sb = env.sbuf("x_res", [128, NT * 1024], F32)
        xregs = [env.reg(f"x{i}") for i in range(NT)]
        C.load(P)
        xr = x_in.rearrange("(i p) d -> p i d", p=128)
        xw = x_sb.rearrange("p (i d) -> p i d", d=1024)
        ldx = P.run("sync", lambda e: e.dma_start(out=xw, in_=xr), dma=True)
        for r in xregs:
            r.w = ldx
        for bi, bn in enumerate(block_names):
            env.scope_id = bi
            env.needs_final_add = False
            with contextlib.ExitStack() as ps_stack:
                outer = env.stack
                env.stack = ps_stack
                env.ps = {}
                for rn in list(env.regs):
                    if rn.startswith("rP") or rn.startswith("rT"):
                        del env.regs[rn]
                if BLOCK_PSUM.get(bn) != "custom":
                    setup_psum_banks(env)
                BLOCK_EMITTERS[bn](P, env, C, x_sb, xregs, W)
                if env.needs_final_add:
                    emit_final_add(P, env, C, x_sb, xregs, None)
                env.stack = outer
            if bi < len(block_names) - 1:
                toks = tuple(P.tok(s) for s in list(P.sems))
                for eng in ("sync", "scalar", "vector", "tensor", "gpsimd"):
                    P._raw_run(eng, lambda e: None, toks, None, 1)
                for r in env.regs.values():
                    r.w = None
                    r.rs = []
                env.reset_block_scope()
        xo = x_out.rearrange("(i p) d -> p i d", p=128)
        st = P.run("sync", lambda e: e.dma_start(out=xo, in_=xw),
                   reads=xregs, dma=True)
        P.run("sync", lambda e: None, reads=[st])
    return nc


def _weight_specs(inputs):
    return {k: (v.shape, str(v.dtype)) for k, v in inputs.items()}


def run_blocks(x_full, inputs, block_names, core_ids=None):
    if core_ids is None:
        core_ids = list(range(NCORES))
    n = len(core_ids)
    cf, cbm = build_host_consts()
    nc = build_block_program(block_names, _weight_specs(inputs))
    needed = set()
    for bn in block_names:
        needed.update(BLOCK_WEIGHTS[bn])
    in_maps = []
    for c in range(n):
        m = {"x": np.ascontiguousarray(
                np.asarray(x_full[c * B_LOC:(c + 1) * B_LOC], np.float32).reshape(NTOK, D)),
             "consts_f": cf, "consts_b": cbm}
        for k in sorted(needed):
            m[k] = np.ascontiguousarray(inputs[k])
        in_maps.append(m)
    res = run_bass_kernel_spmd(nc, in_maps, core_ids=core_ids)
    outs = [r["x_out"].reshape(B_LOC, S, D) for r in res.results]
    return np.concatenate(outs, axis=0)


BLOCK_CHAIN = [["l0_attn", "l0_ffn", "l1_attn", "l1_ffn"],
               ["l2_attn", "l2_ffn", "l3_attn", "l3_ffn"]]


def kernel(**inputs):
    x = np.asarray(inputs["L0_attn_x"], np.float32)
    for grp in BLOCK_CHAIN:
        x = run_blocks(x, inputs, grp)
    return x
